# version 34
# baseline (speedup 1.0000x reference)
import numpy as np
import ml_dtypes
import concourse.bass as bass
import concourse.mybir as mybir
from concourse.bass_utils import run_bass_kernel_spmd

F32 = mybir.dt.float32
BF16 = mybir.dt.bfloat16
I32 = mybir.dt.int32
AF = mybir.ActivationFunctionType
ALU = mybir.AluOpType
AX = mybir.AxisListType.X
ts = bass.ts
ds = bass.ds

D = 1024
T = 2048
EC = 8
TTILES = 16
NH = 16
DH = 64
HID = 3072
HCH = 24
HG = 3
NG = HCH // HG
INV127 = 1.0 / 127.0
EPS = 1e-5

ENGS = ("sync", "gpsimd", "tensor", "vector", "scalar")


class Prog:

    DMA_SEMS = ("dc", "dx", "dx2", "dst", "dl0", "dl1", "dl2", "dl3",
                "dr0", "dr1", "dr2", "dr3", "dw", "dp0", "dp1", "dp2",
                "dp3")

    def __init__(self, nc):
        self.nc = nc
        self.queues = {e: [] for e in ENGS}
        self.muted = False
        self.sems = {}
        self.marks = {}
        for e in ENGS:
            self.sems[e] = [nc.alloc_semaphore(f"sem_{e}"), 0]
        for e in self.DMA_SEMS:
            self.sems[e] = [nc.alloc_semaphore(f"sem_{e}"), 0]

    def op(self, eng, fn, inc=False):
        if self.muted:
            return None
        if inc:
            s = self.sems[eng]
            s[1] += 1
            val, h = s[1], s[0]
            self.queues[eng].append(lambda e: fn(e).then_inc(h, 1))
            return (eng, val)
        self.queues[eng].append(fn)
        return None

    def wait_self(self, eng, tok):
        if tok is None or self.muted:
            return
        sem_name, val = tok
        h = self.sems[sem_name][0]
        self.queues[eng].append(lambda e: e.wait_ge(h, val))

    def wait(self, eng, tok):
        if tok is None or self.muted:
            return
        sem_name, val = tok
        if sem_name == eng:
            return
        key = (eng, sem_name)
        if self.marks.get(key, 0) >= val:
            return
        self.marks[key] = val
        h = self.sems[sem_name][0]
        self.queues[eng].append(lambda e: e.wait_ge(h, val))

    def dma(self, queue, out, in_, sem="dst", **kw):
        if self.muted:
            return None
        sem_name = sem
        s = self.sems[sem_name]
        s[1] += 16
        val, h = s[1], s[0]
        self.queues[queue].append(
            lambda e: e.dma_start(out=out, in_=in_, **kw).then_inc(h, 16))
        return (sem_name, val)

    @staticmethod
    def _nop(en):
        en.drain()
        return en.nop()

    def barrier(self):
        if self.muted:
            return
        dma_toks = [(d, self.sems[d][1]) for d in self.DMA_SEMS]
        for e in ENGS:
            for t in dma_toks:
                if t[1] > 0:
                    self.wait(e, t)
        toks = []
        for e in ENGS:
            s = self.sems[e]
            s[1] += 1
            h, val = s[0], s[1]
            self.queues[e].append(
                lambda en, h=h: Prog._nop(en).then_inc(h, 1))
            toks.append((e, val))
        for e in ENGS:
            for t in toks:
                self.wait(e, t)

    def self_sync(self, eng):
        if self.muted:
            return
        s = self.sems[eng]
        s[1] += 1
        h, val = s[0], s[1]
        self.queues[eng].append(lambda e: Prog._nop(e).then_inc(h, 1))
        self.queues[eng].append(lambda e: e.wait_ge(h, val))

    def replay(self):
        nc = self.nc
        with nc.Block() as block:
            @block.sync
            def _(e):
                for fn in self.queues["sync"]:
                    fn(e)

            @block.gpsimd
            def _(e):
                for fn in self.queues["gpsimd"]:
                    fn(e)

            @block.tensor
            def _(e):
                for fn in self.queues["tensor"]:
                    fn(e)

            @block.vector
            def _(e):
                for fn in self.queues["vector"]:
                    fn(e)

            @block.scalar
            def _(e):
                for fn in self.queues["scalar"]:
                    fn(e)


class Arena:
    def __init__(self, nc, size_bytes):
        self.nc = nc
        self.handle = nc.alloc_sbuf_tensor("arena", [128, size_bytes],
                                           mybir.dt.uint8)
        self.base = nc.lookup_mloc(self.handle).addr
        self.size = size_bytes

    def place(self, name, shape, dtype, off):
        nbytes = int(np.prod(shape[1:])) * mybir.dt.size(dtype)
        assert off + nbytes <= self.size, (name, off, nbytes)
        assert off % 32 == 0, (name, off)
        return self.nc.alloc_sbuf_tensor_at(name, list(shape), dtype,
                                            offset=self.base + off)


def make_constants():
    c = {}
    c["ident_bf"] = np.eye(128, dtype=ml_dtypes.bfloat16)
    c["ones_bf"] = np.ones((128, 128), dtype=ml_dtypes.bfloat16)
    c["negones_bf"] = np.full((128, 128), -1.0, dtype=ml_dtypes.bfloat16)
    c["onescol_f32"] = np.ones((1, 128), dtype=np.float32)
    c["onescol8_bf"] = np.ones((8, 1), dtype=ml_dtypes.bfloat16)
    c["onesrow_bf"] = np.ones((1, 512), dtype=ml_dtypes.bfloat16)
    c["tri_bf"] = np.tril(np.ones((128, 128))).astype(ml_dtypes.bfloat16)
    trineg = np.where(np.tril(np.ones((128, 128), dtype=bool)), 0.0, -1e30)
    c["trineg_bf"] = trineg.astype(ml_dtypes.bfloat16)
    c["trinegT_bf"] = trineg.T.astype(ml_dtypes.bfloat16)
    c["neg1col"] = np.full((128, 1), -1.0, dtype=np.float32)
    o64 = np.zeros((128, 2), dtype=ml_dtypes.bfloat16)
    o64[0:64, 0] = 1.0
    o64[64:128, 1] = 1.0
    c["ones64_bf"] = o64
    j = np.arange(T, dtype=np.float32)[:, None]
    i = np.arange(DH // 2, dtype=np.float32)[None, :]
    expo = (-2.0 * i / np.float32(DH)).astype(np.float32)
    theta = j * np.power(np.float32(10000.0), expo)
    th64 = theta.astype(np.float64)
    m = (np.cos(th64) + np.sin(th64)).astype(np.float32)
    mfull = np.concatenate([m, m], axis=1)
    Mk = np.empty((128, T), dtype=np.float32)
    for r in range(128):
        Mk[r, :] = mfull[:, r % 64]
    c["Mk"] = Mk
    c["Mq"] = (Mk * np.float32(0.125)).astype(np.float32)
    c["epscol"] = np.full((128, 1), EPS, dtype=np.float32)
    return c


def build(taps=(), upto=99, n7q=4, n7t=4):
    nc = bass.Bass("TRN2", target_bir_lowering=False)
    taps = set(taps)

    xT_d = nc.dram_tensor("xT", [D, T], F32, kind="ExternalInput").ap()
    Wq0_d = nc.dram_tensor("Wq0", [D, D], F32, kind="ExternalInput").ap()
    Wk0_d = nc.dram_tensor("Wk0", [D, D], F32, kind="ExternalInput").ap()
    Wv0_d = nc.dram_tensor("Wv0", [D, D], F32, kind="ExternalInput").ap()
    Wo0_d = nc.dram_tensor("Wo0", [D, D], F32, kind="ExternalInput").ap()
    bL0_d = nc.dram_tensor("bL0", [128, 4, EC], F32, kind="ExternalInput").ap()
    wg_d = nc.dram_tensor("wg", [HID, D], I32, kind="ExternalInput").ap()
    wu_d = nc.dram_tensor("wu", [HID, D], I32, kind="ExternalInput").ap()
    wd_d = nc.dram_tensor("wd", [D, HID], I32, kind="ExternalInput").ap()
    Wq1_d = nc.dram_tensor("Wq1", [D, D], F32, kind="ExternalInput").ap()
    Wk1_d = nc.dram_tensor("Wk1", [D, D], F32, kind="ExternalInput").ap()
    Wv1_d = nc.dram_tensor("Wv1", [D, D], F32, kind="ExternalInput").ap()
    Wo1_d = nc.dram_tensor("Wo1", [D, D], F32, kind="ExternalInput").ap()
    bL1_d = nc.dram_tensor("bL1", [128, 4, EC], F32, kind="ExternalInput").ap()
    Wsel_d = nc.dram_tensor("Wsel", [8, D, D], F32, kind="ExternalInput").ap()
    bsel_d = nc.dram_tensor("bsel", [8, D], F32, kind="ExternalInput").ap()
    w8_d = nc.dram_tensor("w8", [1, 8], F32, kind="ExternalInput").ap()
    out_d = nc.dram_tensor("outT", [D, T], F32, kind="ExternalOutput").ap()
    x1park_d = nc.dram_tensor("x1park", [D, T], F32, kind="Internal").ap()
    statspark_d = nc.dram_tensor("statspark", [2, 128, T], F32, kind="Internal").ap()
    rowpark_d = nc.dram_tensor("rowpark", [T], F32, kind="Internal").ap()

    C = make_constants()
    consts_dram = {k: nc.inline_tensor(v, k + "_c").ap() for k, v in C.items()}

    tap_d = {}

    def tap(name, shape):
        if name in taps:
            tap_d[name] = nc.dram_tensor("tap_" + name, list(shape), F32,
                                         kind="ExternalOutput").ap()
            return tap_d[name]
        return None

    ident = nc.alloc_sbuf_tensor("ident", [128, 128], BF16)
    ones = nc.alloc_sbuf_tensor("ones", [128, 128], BF16)
    negones = nc.alloc_sbuf_tensor("negones", [128, 128], BF16)
    onescol = nc.alloc_sbuf_tensor("onescol", [1, 128], F32)
    onescol8 = nc.alloc_sbuf_tensor("onescol8", [8, 1], BF16)
    onesrow = nc.alloc_sbuf_tensor("onesrow", [1, 512], BF16)
    tri = nc.alloc_sbuf_tensor("tri", [128, 128], BF16)
    trineg = nc.alloc_sbuf_tensor("trineg", [128, 128], BF16)
    trinegT = nc.alloc_sbuf_tensor("trinegT", [128, 128], BF16)
    neg1col = nc.alloc_sbuf_tensor("neg1col", [128, 1], F32)
    negBh = nc.alloc_sbuf_tensor("negBh", [128, NH], F32)
    ones64 = nc.alloc_sbuf_tensor("ones64", [128, 2], BF16)
    hnorm = nc.alloc_sbuf_tensor("hnorm", [2, 2, EC], F32)
    epscol = nc.alloc_sbuf_tensor("epscol", [128, 1], F32)
    bL0 = nc.alloc_sbuf_tensor("bL0_sb", [128, 4, EC], F32)
    bL1 = nc.alloc_sbuf_tensor("bL1_sb", [128, 4, EC], F32)
    w8sb = nc.alloc_sbuf_tensor("w8sb", [1, 8], F32)
    w8col = nc.alloc_sbuf_tensor("w8col", [128, 8], F32)
    bmixcol = nc.alloc_sbuf_tensor("bmixcol", [128, EC], F32)
    negmax = nc.alloc_sbuf_tensor("negmax", [128, 2], F32)
    sumcol = nc.alloc_sbuf_tensor("sumcol", [128, 2], F32)
    reccol = nc.alloc_sbuf_tensor("reccol", [128, 2], F32)
    Sm = nc.alloc_sbuf_tensor("Sm", [128, 2, 8], F32)
    nqsq = nc.alloc_sbuf_tensor("nqsq", [128, TTILES], F32)
    bq2col = nc.alloc_sbuf_tensor("bq2col", [128, EC], F32)

    nc.const_aps.register(F32, EPS, epscol[:, 0:1])

    oQ, oK, oV, oG, oXN = 0, 32 << 10, 64 << 10, 96 << 10, 128 << 10
    oW = 160 << 10
    oMISC = 200 << 10
    AR = 204 * 1024
    ar = Arena(nc, AR)

    Q = ar.place("Q", [128, EC, T], BF16, oQ)
    K = ar.place("K", [128, EC, T], BF16, oK)
    V = ar.place("V", [128, TTILES, D], BF16, oV)
    G = ar.place("G", [128, EC, T], BF16, oG)
    XN = ar.place("XN", [128, EC, T], BF16, oXN)
    ISCR = ar.place("ISCR", [128, 4, D], I32, oXN)

    XF = ar.place("XF", [128, EC, T], F32, oQ)
    XBF = ar.place("XBF", [128, EC, T], BF16, oG)
    XSQ = ar.place("XSQ", [128, EC, T], BF16, oV)
    MB = ar.place("MB", [128, T], F32, oW)
    SQM = ar.place("SQM", [128, T], F32, oW + (8 << 10))
    VARB = ar.place("VARB", [128, T], F32, oW + (16 << 10))
    RSTD = ar.place("RSTD", [128, T], F32, oW + (24 << 10))

    WGT = ar.place("WGT", [128, HG, EC * 128], BF16, oW)
    WUT = ar.place("WUT", [128, HG, EC * 128], BF16, oW + (6 << 10))
    WDT = ar.place("WDT", [128, EC, HG * 128], BF16, oW + (12 << 10))
    WGT2 = ar.place("WGT2", [128, HG, EC * 128], BF16, oV)
    WUT2 = ar.place("WUT2", [128, HG, EC * 128], BF16, oV + (6 << 10))
    WDT2 = ar.place("WDT2", [128, EC, HG * 128], BF16, oV + (12 << 10))
    HT = ar.place("HT", [128, HG, T], BF16, oW + (18 << 10))
    HGB = ar.place("HGB", [128, T], BF16, oW + (30 << 10))
    LANDB = ar.place("LANDB", [128, 2, D], BF16, oW + (34 << 10))
    ACC = ar.place("ACC", [128, EC, T], F32, oQ)
    LAND = ar.place("LAND", [128, 2, D], BF16, oMISC)

    WBLK8 = ar.place("WBLK8", [128, 2, EC, 128], BF16, oW)
    PRJ = ar.place("PRJ", [128, T], F32, oW + (4 << 10))
    VFM = ar.place("VFM", [128, T], BF16, oW + (12 << 10))
    MQT = ar.place("MQT", [128, T], F32, oW + (16 << 10))
    MKT = ar.place("MKT", [128, T], F32, oW + (24 << 10))
    LANDC = ar.place("LANDC", [128, 2, D], BF16, oW + (32 << 10))
    FLND = ar.place("FLND", [128, 2, 512], F32, oW + (36 << 10))

    WOB = ar.place("WOB", [128, EC, D], BF16, oW)
    EB2 = ar.place("EB2", [128, 2, T], BF16, oW + (16 << 10))
    ETB = ar.place("ETB", [128, 2, TTILES, 128], BF16, oW + (24 << 10))
    YTB = ar.place("YTB", [128, EC, 512], BF16, oW + (32 << 10))
    YSB = ar.place("YSB", [128, 2, DH], BF16, oMISC)
    YSB1 = ar.place("YSB1", [128, 2, D], BF16, oMISC)
    XT4 = ar.place("XT4", [128, EC, 512], F32, oXN)
    SCH = ar.place("SCH", [128, 2, 512], F32, oXN + (16 << 10))
    X1BF = ar.place("X1BF", [128, 2, 512], BF16, oXN + (20 << 10))
    X1SQ = ar.place("X1SQ", [128, 2, 512], BF16, oXN + (22 << 10))

    XN1 = ar.place("XN1", [128, EC, T], BF16, oXN)
    X1CH = ar.place("X1CH", [128, 4, T], F32, oQ)
    WMIXT = ar.place("WMIXT", [128, EC, D], BF16, oW)
    PLAND32 = ar.place("PLAND32", [128, 4, D], F32, oG)
    identw8 = ar.place("identw8", [128, 8, 128], F32, oG + (16 << 10))
    BS8 = ar.place("BS8", [8, D], F32, oW + (16 << 10))
    BS8B = ar.place("BS8B", [8, D], BF16, oW + (20 << 10))
    W1FULL = ar.place("W1FULL", [128, EC, D], BF16, oW + (16 << 10))
    PRJ5 = ar.place("PRJ5", [128, T], F32, oW + (32 << 10))
    VFM5 = ar.place("VFM5", [128, T], BF16, oMISC)
    PLAND = ar.place("PLAND", [128, 2, D], BF16, oQ)

    NQSQ_ROW = ar.place("NQSQ_ROW", [1, T], BF16, oXN)
    NKSQ_ROW = ar.place("NKSQ_ROW", [1, T], BF16, oXN + (4 << 10))
    SSB = ar.place("SSB", [128, T], F32, oXN + (8 << 10))
    SCH7 = ar.place("SCH7", [128, 2, 512], F32, oXN + (8 << 10))
    XT7 = ar.place("XT7", [128, EC, 512], F32, oXN + (16 << 10))

    PS = nc.alloc_psum_tensor("PS", [128, 4096], F32)

    def BK(i, lo=0, hi=512):
        return PS[:, i * 512 + lo:i * 512 + hi]

    P = Prog(nc)

    for dst, key in ((ident, "ident_bf"), (ones, "ones_bf"),
                     (negones, "negones_bf"), (onescol, "onescol_f32"),
                     (onescol8, "onescol8_bf"), (onesrow, "onesrow_bf"),
                     (tri, "tri_bf"),
                     (trineg, "trineg_bf"), (trinegT, "trinegT_bf"),
                     (neg1col, "neg1col"), (ones64, "ones64_bf"),
                     (epscol, "epscol")):
        P.dma("sync", dst[:], consts_dram[key], sem="dc")
    P.dma("sync", bL0[:], bL0_d, sem="dc")
    P.dma("sync", bL1[:], bL1_d, sem="dc")
    t_w8 = P.dma("sync", w8sb[:], w8_d, sem="dc")
    pl_dma = []
    for pi in range(4):
        wsrc_ = wg_d if pi < 3 else wu_d
        hc0 = pi if pi < 3 else 0
        pl_dma.append(P.dma("sync", ISCR[:, pi, :],
                            wsrc_[hc0 * 128:(hc0 + 1) * 128, :],
                            sem=f"dp{pi}"))
    preland = []
    for pi in range(4):
        P.wait("vector", pl_dma[pi])
        preland.append(P.op("vector", lambda v, pi=pi: v.tensor_copy(
            (LAND[:, pi, :] if pi < 2 else LANDB[:, pi - 2, :]),
            ISCR[:, pi, :]), inc=True))
    t_x_chunks = []
    for c in range(EC):
        if c >= 4:
            P.wait("sync", t_x_chunks[c - 4])
        t_x_chunks.append(P.dma(
            "sync", XF[:, c, :],
            xT_d.rearrange("(c p) t -> p c t", p=128)[:, c, :],
            sem=f"dr{c % 4}"))
    t_x = t_x_chunks[-1]
    P.wait("tensor", t_w8)
    tk = P.op("tensor", lambda t: t.matmul(BK(7, 0, 8), onescol[0:1, :],
                                           w8sb[0:1, :], start=True,
                                           stop=True), inc=True)
    P.wait("vector", tk)
    t_w8c = P.op("vector", lambda v: v.tensor_copy(w8col[:], BK(7, 0, 8)),
                 inc=True)
    P.wait("tensor", t_w8c)

    def emit_layernorm(in_f32, xbf, xsq, xn_out, dep_toks):
        for c in range(EC):
            tk_ = dep_toks[c] if len(dep_toks) == EC else dep_toks[-1]
            P.wait("vector", tk_)
            cp = P.op("vector", lambda v, c=c: v.tensor_copy(
                xbf[:, c, :], in_f32[:, c, :]), inc=True)
            P.wait("scalar", tk_)
            sq = P.op("scalar", lambda s, c=c: s.activation(
                xsq[:, c, :], in_f32[:, c, :], AF.Square), inc=True)
            P.wait("tensor", cp)
            P.wait("tensor", sq)
            for n in range(4):
                P.op("tensor", lambda t, c=c, n=n: t.matmul(
                    BK(n), ones[:, :], xbf[:, c, ts(n, 512)],
                    start=(c == 0), stop=(c == EC - 1)),
                    inc=(c == EC - 1 and n == 3))
            for n in range(4):
                P.op("tensor", lambda t, c=c, n=n: t.matmul(
                    BK(4 + n), ones[:, :], xsq[:, c, ts(n, 512)],
                    start=(c == 0), stop=(c == EC - 1)),
                    inc=(c == EC - 1 and n == 3))
        tk_sq = ("tensor", P.sems["tensor"][1])
        tk_sum = ("tensor", P.sems["tensor"][1] - 1)
        P.wait("vector", tk_sum)
        tk_mb = None
        for n in range(4):
            tk_mb = P.op("vector", lambda v, n=n: v.tensor_scalar_mul(
                MB[:, ts(n, 512)], BK(n), 1.0 / D), inc=(n == 3))
        P.wait("scalar", tk_mb)
        t_sqm = P.op("scalar", lambda s: s.activation(
            SQM[:, :], MB[:, :], AF.Square), inc=True)
        P.wait("vector", tk_sq)
        P.wait("vector", t_sqm)
        tk_var = None
        for n in range(4):
            tk_var = P.op("vector", lambda v, n=n: v.scalar_tensor_tensor(
                VARB[:, ts(n, 512)], BK(4 + n), 1.0 / D,
                SQM[:, ts(n, 512)], ALU.mult, ALU.subtract), inc=(n == 3))
        P.wait("scalar", tk_var)
        tLn0 = P.op("scalar", lambda s: s.activation(
            VARB[:, :], VARB[:, :], AF.Ln, bias=EPS), inc=True)
        P.wait_self("scalar", tLn0)
        t_rstd = P.op("scalar", lambda s: s.activation(
            RSTD[:, :], VARB[:, :], AF.Exp, scale=-0.5), inc=True)
        P.wait("vector", t_rstd)
        sub_toks = {}
        tok = None
        for c in range(EC):
            sub_toks[c] = P.op("vector", lambda v, c=c: v.tensor_sub(
                in_f32[:, c, :], in_f32[:, c, :], MB[:, :]), inc=True)
            if c >= 1:
                P.wait_self("vector", sub_toks[c - 1])
                tok = P.op("vector", lambda v, c=c - 1: v.tensor_mul(
                    xn_out[:, c, :], in_f32[:, c, :], RSTD[:, :]), inc=True)
        P.wait_self("vector", sub_toks[EC - 1])
        tok = P.op("vector", lambda v: v.tensor_mul(
            xn_out[:, EC - 1, :], in_f32[:, EC - 1, :], RSTD[:, :]),
            inc=True)
        return tok, tk_var

    def dump_fm(tp_ap, src, nchunks=EC, via=None, cvt=True):
        P.barrier()
        for c in range(nchunks):
            if cvt:
                tkc = P.op("vector", lambda v, c=c: v.tensor_copy(
                    via[:, :], src[:, c, :]), inc=True)
                P.wait("sync", tkc)
                t_ = P.dma("sync",
                           tp_ap.rearrange("(c p) t -> p c t", p=128)[:, c, :],
                           via[:, :])
                P.wait("vector", t_)
            else:
                t_ = P.dma("sync",
                           tp_ap.rearrange("(c p) t -> p c t", p=128)[:, c, :],
                           src[:, c, :])
        P.barrier()

    tk_xn, tk_var0 = emit_layernorm(XF, XBF, XSQ, XN, t_x_chunks)
    tp = tap("xn", [D, T])
    if tp is not None:
        dump_fm(tp, XN, via=PRJ)

    P.muted = P.muted or (upto < 2)

    prev_scalar_read = None
    land_guard2 = {0: None, 1: None, 2: None, 3: None}

    def LANDX(li):
        return LAND[:, li, :] if li < 2 else LANDB[:, li - 2, :]

    bev67 = {6: None, 7: None}
    p2flip = [0]

    evac_force = [None]

    def p2_evac(dst_ap_fn, bk6, ln):
        if evac_force[0] is not None:
            eng = evac_force[0]
        else:
            eng = "vector" if p2flip[0] == 0 else "scalar"
            p2flip[0] ^= 1
        if eng == "vector":
            tok = P.op("vector", lambda v: v.tensor_copy(
                dst_ap_fn(), BK(bk6, 0, ln)), inc=True)
        else:
            tok = P.op("scalar", lambda s: s.copy(
                dst_ap_fn(), BK(bk6, 0, ln)), inc=True)
        bev67[bk6] = tok
        return tok
    tr_evac_guard = None
    ht_tok = None
    WB = [(WGT, WUT, WDT), (WGT2, WUT2, WDT2)]
    prep_done = {}
    ht_cur = [None]

    def p2_slices(g):
        out = []
        for w_i in range(2):
            wsrc = (wg_d, wu_d)[w_i]
            for hl in range(HG):
                hc = g * HG + hl
                out.append(("gu", w_i, hl,
                            lambda li, wsrc=wsrc, hc=hc: P.dma(
                                "gpsimd", LANDX(li),
                                wsrc[hc * 128:(hc + 1) * 128, :],
                                sem=f"dl{li}")))
        for dc in range(EC):
            out.append(("wd", 0, dc,
                        lambda li, dc=dc, g=g: P.dma(
                            "gpsimd", LANDX(li)[:, 0:HG * 128],
                            wd_d[dc * 128:(dc + 1) * 128,
                                 g * HG * 128:(g + 1) * HG * 128],
                            sem=f"dl{li}")))
        return out

    def prep_issue4(g):
        toks = []
        sl = p2_slices(g)
        for i in range(4):
            li = i % 4
            P.wait("gpsimd", land_guard2[li])
            toks.append(sl[i][3](li))
        return toks

    def prep_gen(g, pre_toks):
        WGTg, WUTg, WDTg = WB[(g + 1) % 2]
        lastv = lasts = None
        sl = p2_slices(g)
        toks = list(pre_toks)
        tparity = 0
        first = True
        for i, (kind, w_i, idx, dma_fn) in enumerate(sl):
            li = i % 4
            P.wait("tensor", toks[i])
            if first:
                P.wait("tensor", ht_cur[0])
                first = False
            if kind == "gu":
                dstbuf = WGTg if w_i == 0 else WUTg
                for b0 in (0, 4):
                    bk6 = 6 + tparity
                    P.wait("tensor", bev67[bk6])
                    tkT = None
                    for u in range(4):
                        tkT = P.op("tensor", lambda t, li=li, j=b0 + u, u=u,
                                   bk6=bk6: t.matmul(
                            BK(bk6, u * 128, (u + 1) * 128),
                            LANDX(li)[:, ts(j, 128)], ident[:, :],
                            start=True, stop=True), inc=(u == 3))
                    land_guard2[li] = tkT
                    eng = evac_force[0] or (
                        "vector" if p2flip[0] == 0 else "scalar")
                    P.wait(eng, tkT)
                    tok = p2_evac(
                        lambda dstbuf=dstbuf, hl=idx, b0=b0:
                        dstbuf[:, hl, b0 * 128:(b0 + 4) * 128], bk6, 512)
                    if eng == "vector":
                        lastv = tok
                    else:
                        lasts = tok
                    tparity ^= 1
            else:
                bk6 = 6 + tparity
                P.wait("tensor", bev67[bk6])
                tkT = None
                for hl in range(HG):
                    tkT = P.op("tensor", lambda t, li=li, hl=hl, bk6=bk6:
                               t.matmul(
                        BK(bk6, hl * 128, (hl + 1) * 128),
                        LANDX(li)[:, ts(hl, 128)], ident[:, :], start=True,
                        stop=True), inc=(hl == HG - 1))
                land_guard2[li] = tkT
                eng = evac_force[0] or (
                    "vector" if p2flip[0] == 0 else "scalar")
                P.wait(eng, tkT)
                tok = p2_evac(lambda dc=idx, WDTg=WDTg:
                              WDTg[:, dc, 0:HG * 128], bk6, HG * 128)
                if eng == "vector":
                    lastv = tok
                else:
                    lasts = tok
                tparity ^= 1
            prep_done[g] = (lastv, lasts)
            if i + 4 < len(sl):
                li4 = (i + 4) % 4
                P.wait("gpsimd", land_guard2[li4])
                toks.append(sl[i + 4][3](li4))
            yield

    ht_cur[0] = tk_var0
    evac_force[0] = "scalar"
    for _ in prep_gen(0, preland):
        pass
    evac_force[0] = None
    prev_evac = [None, None]
    for g in range(NG):
        WGTg, WUTg, WDTg = WB[(g + 1) % 2]
        if g == 0:
            P.wait("tensor", tk_xn)
        P.wait("tensor", prep_done[g][0])
        P.wait("tensor", prep_done[g][1])
        P.wait("tensor", prev_evac[0])
        P.wait("tensor", prev_evac[1])
        pre4 = prep_issue4(g + 1) if g + 1 < NG else None
        ht_tok = None
        for hl in range(HG):
            P.wait("tensor", prev_scalar_read)
            for j in range(EC):
                for n in range(4):
                    P.op("tensor", lambda t, hl=hl, j=j, n=n, WGTg=WGTg:
                         t.matmul(
                        BK(n), WGTg[:, hl, ts(j, 128)], XN[:, j, ts(n, 512)],
                        start=(j == 0), stop=(j == EC - 1)),
                        inc=(j == EC - 1 and n == 3))
            tg = ("tensor", P.sems["tensor"][1])
            P.wait("tensor", ht_tok)
            for j in range(EC):
                for n in range(4):
                    P.op("tensor", lambda t, hl=hl, j=j, n=n, WUTg=WUTg:
                         t.matmul(
                        BK(4 + n), WUTg[:, hl, ts(j, 128)],
                        XN[:, j, ts(n, 512)], start=(j == 0),
                        stop=(j == EC - 1)), inc=(j == EC - 1 and n == 3))
            tu = ("tensor", P.sems["tensor"][1])
            P.wait("scalar", tg)
            P.wait("scalar", ht_tok)
            tkg = None
            for n in range(4):
                tkg = P.op("scalar", lambda s, n=n: s.activation(
                    HGB[:, ts(n, 512)], BK(n), AF.Silu, scale=INV127),
                    inc=(n == 3))
            P.wait("vector", tkg)
            P.wait("vector", tu)
            ht_tok = None
            for n in range(4):
                ht_tok = P.op("vector", lambda v, hl=hl, n=n:
                              v.scalar_tensor_tensor(
                                  HT[:, hl, ts(n, 512)], BK(4 + n), INV127,
                                  HGB[:, ts(n, 512)], ALU.mult, ALU.mult),
                              inc=True)
            prev_scalar_read = tkg
        ht_cur[0] = ht_tok
        P.wait("tensor", ht_tok)
        gen = prep_gen(g + 1, pre4) if g + 1 < NG else None
        evac_toks = [None, None]
        for ui, (dc, n) in enumerate([(dc, n) for dc in range(EC)
                                      for n in range(4)]):
            bki = ui % 2
            P.wait("tensor", evac_toks[bki])
            tkd = None
            for hl in range(HG):
                tkd = P.op("tensor", lambda t, dc=dc, n=n, hl=hl, bki=bki,
                           WDTg=WDTg:
                           t.matmul(BK(bki), WDTg[:, dc, ts(hl, 128)],
                                    HT[:, hl, ts(n, 512)],
                                    start=(hl == 0), stop=(hl == HG - 1)),
                           inc=(hl == HG - 1))
            P.wait("vector", tkd)
            if g == 0:
                tke = P.op("vector", lambda v, dc=dc, n=n, bki=bki:
                           v.tensor_copy(ACC[:, dc, ts(n, 512)], BK(bki)),
                           inc=True)
            else:
                tke = P.op("vector", lambda v, dc=dc, n=n, bki=bki:
                           v.tensor_add(ACC[:, dc, ts(n, 512)],
                                        ACC[:, dc, ts(n, 512)], BK(bki)),
                           inc=True)
            evac_toks[bki] = tke
            last_acc_tok = tke
            if gen is not None and ui % 2 == 1:
                try:
                    next(gen)
                except StopIteration:
                    gen = None
        while gen is not None:
            try:
                next(gen)
            except StopIteration:
                gen = None
        prev_evac = list(evac_toks)
    P.wait("scalar", last_acc_tok)
    tkgG = None
    for dc in range(EC):
        tkgG = P.op("scalar", lambda s, dc=dc: s.activation(
            G[:, dc, :], ACC[:, dc, :], AF.Silu, scale=INV127),
            inc=(dc == EC - 1))
    tp = tap("g", [D, T])
    if tp is not None:
        P.wait("vector", tkgG)
        dump_fm(tp, G, via=PRJ)

    P.barrier()
    P.muted = P.muted or (upto < 3)

    t_mq = P.dma("sync", MQT[:, :], consts_dram["Mq"], sem="dc")
    t_mk = P.dma("sync", MKT[:, :], consts_dram["Mk"], sem="dc")

    land_guard = {0: None, 1: None, 2: None, 3: None}
    cvt67 = {0: None, 1: None}
    g45 = {4: None, 5: None}

    def emit_proj_L0(W_d, bias_idx, kind):
        outs = {"q": Q, "k": K}
        wblk_evac = [None, None]
        wblk_read = [None, None]

        def LANDY(li):
            return LAND[:, li, :] if li < 2 else LANDC[:, li - 2, :]

        def stage_T(fc):
            li = fc % 4
            sl = fc % 2
            cvt = None
            for h in range(2):
                P.wait("sync", cvt67[h])
                t_half = P.dma("sync", FLND[:, h, :],
                               W_d[fc * 128:(fc + 1) * 128,
                                   h * 512:(h + 1) * 512], sem=f"dp{h}")
                P.wait("scalar", t_half)
                P.wait("scalar", land_guard[li])
                cvt = P.op("scalar", lambda s, li=li, h=h: s.copy(
                    LANDY(li)[:, h * 512:(h + 1) * 512], FLND[:, h, :]),
                    inc=True)
                cvt67[h] = cvt
            t_land = cvt
            P.wait("tensor", t_land)
            P.wait("tensor", wblk_read[sl])
            for bi, b0 in enumerate((0, 4)):
                bk = 4 + bi
                P.wait("tensor", g45[bk])
                tkT = None
                for u in range(4):
                    tkT = P.op("tensor", lambda t, li=li, j=b0 + u, u=u,
                               bk=bk: t.matmul(
                        BK(bk, u * 128, (u + 1) * 128), LANDY(li)[:, ts(j, 128)],
                        ident[:, :], start=True, stop=True), inc=(u == 3))
                land_guard[li] = tkT
                P.wait("vector", tkT)
                g45[bk] = P.op("vector", lambda v, sl=sl, b0=b0, bk=bk:
                               v.tensor_copy(WBLK8[:, sl, b0:b0 + 4, :],
                                             BK(bk, 0, 512)), inc=True)
            wblk_evac[sl] = g45[5]

        def stage_M(fc):
            sl = fc % 2
            P.wait("tensor", wblk_evac[sl])
            tkM = None
            for j in range(EC):
                for n in range(4):
                    tkM = P.op("tensor", lambda t, sl=sl, j=j, n=n: t.matmul(
                        BK(n), WBLK8[:, sl, j, :], XN[:, j, ts(n, 512)],
                        start=(j == 0), stop=(j == EC - 1)),
                        inc=(j == EC - 1 and n == 3))
            wblk_read[sl] = tkM
            P.wait("scalar", tkM)
            tkP = None
            for n in range(4):
                tkP = P.op("scalar", lambda s, n=n, fc=fc: s.activation(
                    PRJ[:, ts(n, 512)], BK(n), AF.Identity,
                    bias=bL0[:, bias_idx, fc:fc + 1]), inc=(n == 3))
            P.wait("vector", tkP)
            if kind in ("q", "k"):
                P.wait("vector", t_mk)
                tbl = MQT if kind == "q" else MKT
                tkR = P.op("vector", lambda v, fc=fc, tbl=tbl: v.tensor_mul(
                    outs[kind][:, fc, :], PRJ[:, :], tbl[:, :]), inc=True)
                P.wait("tensor", tkR)
                P.wait("scalar", tkR)
            else:
                tkC = P.op("vector", lambda v: v.tensor_copy(
                    VFM[:, :], PRJ[:, :]), inc=True)
                P.wait("tensor", tkC)
                P.wait("scalar", tkC)
                for bi, b0 in enumerate(range(0, TTILES, 4)):
                    bk = 4 + (bi % 2)
                    P.wait("tensor", g45[bk])
                    tkT = None
                    for u in range(4):
                        tkT = P.op("tensor", lambda t, b0=b0, u=u, bk=bk:
                                   t.matmul(BK(bk, u * 128, (u + 1) * 128),
                                            VFM[:, ts(b0 + u, 128)],
                                            ident[:, :], start=True,
                                            stop=True), inc=(u == 3))
                    P.wait("vector", tkT)
                    g45[bk] = P.op("vector", lambda v, b0=b0, fc=fc, bk=bk:
                                   v.tensor_copy(V[:, b0:b0 + 4, ts(fc, 128)],
                                                 BK(bk, 0, 512).rearrange(
                                                     "p (a b) -> p a b", a=4)),
                                   inc=True)

        stage_T(0)
        for fc in range(EC):
            if fc + 1 < EC:
                stage_T(fc + 1)
            stage_M(fc)

    emit_proj_L0(Wk0_d, 1, "k")
    emit_proj_L0(Wv0_d, 2, "v")
    emit_proj_L0(Wq0_d, 0, "q")
    for nm, buf in (("q0", Q), ("k0", K)):
        tp = tap(nm, [D, T])
        if tp is not None:
            dump_fm(tp, buf, via=PRJ)
    tp = tap("v0", [T, D])
    if tp is not None:
        dump_fm(tp, V, nchunks=TTILES, via=PRJ[:, 0:D].rearrange("p t -> p t"))
    P.barrier()
    P.muted = P.muted or (upto < 4)

    for fc in range(EC):
        li = fc % 2
        P.wait("gpsimd", land_guard[li])
        t_land = P.dma("gpsimd", LAND[:, li, :],
                       Wo0_d[fc * 128:(fc + 1) * 128, :], sem=f"dl{li}")
        P.wait("tensor", t_land)
        for bi, b0 in enumerate((0, 4)):
            bk = 4 + bi
            P.wait("tensor", g45[bk])
            tkT = None
            for u in range(4):
                tkT = P.op("tensor", lambda t, li=li, j=b0 + u, u=u, bk=bk:
                           t.matmul(BK(bk, u * 128, (u + 1) * 128),
                                    LAND[:, li, ts(j, 128)], ident[:, :],
                                    start=True, stop=True), inc=(u == 3))
            land_guard[li] = tkT
            P.wait("vector", tkT)
            g45[bk] = P.op("vector", lambda v, b0=b0, fc=fc, bk=bk:
                           v.tensor_copy(WOB[:, b0:b0 + 4, ts(fc, 128)],
                                         BK(bk, 0, 512).rearrange(
                                             "p (a b) -> p a b", a=4)),
                           inc=True)
    P.barrier()

    tSqP = {0: None, 1: None}
    tRed = {0: None, 1: None}
    tEBg = None
    for i, (si, c) in enumerate([(si, c) for si in range(2)
                                 for c in range(EC)]):
        srcb = (Q, K)[si]
        sl = i % 2
        P.wait("scalar", tSqP[sl])
        tSq = P.op("scalar", lambda s, c=c, srcb=srcb, sl=sl: s.activation(
            EB2[:, sl, :], srcb[:, c, :], AF.Square), inc=True)
        P.wait("tensor", tSq)
        P.wait("tensor", tRed[sl])
        tmm = None
        for n in range(4):
            tmm = P.op("tensor", lambda t, n=n, sl=sl: t.matmul(
                BK(4 * sl + n)[0:2, :], ones64[:, :], EB2[:, sl, ts(n, 512)],
                start=True, stop=True), inc=(n == 3))
        tSqP[sl] = tmm
        P.wait("vector", tmm)
        tEBg = P.op("vector", lambda v, si=si, c=c, sl=sl: v.reduce_max(
            hnorm[0:2, si, c:c + 1],
            PS[0:2, sl * 2048:sl * 2048 + 2048], axis=AX), inc=True)
        tRed[sl] = tEBg
    P.wait("vector", tRed[0])
    P.wait("vector", tRed[1])
    P.wait_self("vector", tEBg)
    tPr = P.op("vector", lambda v: v.tensor_mul(
        hnorm[0:2, 0, :], hnorm[0:2, 0, :], hnorm[0:2, 1, :]), inc=True)
    P.wait("scalar", tPr)
    tLn = P.op("scalar", lambda s: s.activation(
        hnorm[0:2, 0, :], hnorm[0:2, 0, :], AF.Ln), inc=True)
    P.wait_self("scalar", tLn)
    tEx = P.op("scalar", lambda s: s.activation(
        hnorm[0:2, 1, :], hnorm[0:2, 0, :], AF.Exp, scale=0.5), inc=True)
    P.wait("vector", tEx)
    tNg = P.op("vector", lambda v: v.tensor_scalar_mul(
        hnorm[0:2, 1, :], hnorm[0:2, 1, :], -1.0), inc=True)
    P.wait("sync", tNg)
    t_r = P.dma("sync", rowpark_d[0:16], hnorm[0:2, 1, :])
    P.wait("sync", t_r)
    t_rb = P.dma("sync", SCH[0:1, 0, 0:16],
                 rowpark_d[0:16].rearrange("(p c) -> c p", p=2), sem="dx2",
                 allow_slow_non_contiguous=True)
    P.wait("tensor", t_rb)
    tBc = P.op("tensor", lambda t: t.matmul(
        BK(0, 0, NH), onescol[0:1, :], SCH[0:1, 0, 0:16], start=True,
        stop=True), inc=True)
    P.wait("vector", tBc)
    tBv = P.op("vector", lambda v: v.tensor_copy(negBh[:, :], BK(0, 0, NH)),
               inc=True)
    P.wait("tensor", tBv)
    P.wait("scalar", tBv)

    def head_q(h, gi):
        r0 = (h % 2) * 64
        return Q[r0:r0 + 64, h // 2, ts(gi, 128)]

    def head_k(h, sl):
        r0 = (h % 2) * 64
        return K[r0:r0 + 64, h // 2, sl]

    st4 = {"tE": None, "tMx": None, "tTr": None, "tTrEvac": None,
           "tTrEvacV": None, "tTrEvacS": None, "tEslot": {0: None, 1: None},
           "tTr2": {0: None, 1: None}, "tY": {0: None, 1: None},
           "tYe": {0: None, 1: None}, "tPV": {0: None, 1: None},
           "tRecDone": {0: None, 1: None}}
    bev4 = {4: None, 5: None}

    st4["tSums"] = {0: None, 1: None}
    st4["tEhalf"] = {0: None, 1: None}

    def attn_frontA(h, gi, slot):
        klen = (gi + 1) * 128
        ntt = klen // 128
        off = slot * 1024 if klen <= 1024 else 0
        if klen <= 1024:
            P.wait("tensor", st4["tEhalf"][slot])
        else:
            P.wait("tensor", st4["tEhalf"][0])
            P.wait("tensor", st4["tEhalf"][1])
        tS = None
        for tt in range(ntt):
            tS = P.op("tensor", lambda t, h=h, gi=gi, tt=tt, off=off: t.matmul(
                PS[:, off + tt * 128:off + (tt + 1) * 128],
                head_k(h, ts(tt, 128)), head_q(h, gi), start=True,
                stop=True), inc=(tt == ntt - 1))
        P.wait("vector", tS)
        tMask = P.op("vector", lambda v, klen=klen, off=off: v.tensor_add(
            PS[:, off + klen - 128:off + klen],
            PS[:, off + klen - 128:off + klen], trinegT[:, :]), inc=True)
        return (klen, ntt, off, tMask)

    def attn_frontB(h, gi, slot, stA):
        klen, ntt, off, tMask = stA
        P.wait("scalar", tMask)
        P.wait("scalar", st4["tTr2"][slot])
        if klen <= 1024:
            tE = P.op("scalar", lambda s, klen=klen, slot=slot, h=h, off=off:
                      s.activation(EB2[:, slot, 0:klen],
                                   PS[:, off:off + klen], AF.Exp,
                                   bias=negBh[:, h:h + 1]), inc=True)
            st4["tEhalf"][slot] = tE
        else:
            tE = P.op("scalar", lambda s, klen=klen, slot=slot, h=h:
                      s.activation(EB2[:, slot, 0:klen], PS[:, 0:klen],
                                   AF.Exp, bias=negBh[:, h:h + 1]), inc=True)
            st4["tEhalf"][0] = tE
            st4["tEhalf"][1] = tE
        st4["tE"] = tE
        st4["tTr2"][slot] = tE
        return (ntt, tE)

    def attn_mid(h, gi, slot, front):
        (ntt, tE) = front
        P.wait("tensor", tE)
        P.wait("tensor", st4["tRecDone"][0])
        P.wait("tensor", st4["tRecDone"][1])
        tSum = None
        for tt in range(ntt):
            tSum = P.op("tensor", lambda t, tt=tt, slot=slot: t.matmul(
                BK(4, slot, slot + 1), EB2[:, slot, ts(tt, 128)],
                ones[:, 0:1], start=(tt == 0), stop=(tt == ntt - 1),
                skip_group_check=True), inc=(tt == ntt - 1))
        st4["tSums"][slot] = tSum
        P.wait("vector", tSum)
        P.wait("vector", st4["tYe"][slot])
        tRec = P.op("vector", lambda v, slot=slot: v.reciprocal(
            reccol[:, slot:slot + 1], BK(4, slot, slot + 1)), inc=True)
        st4["tRecDone"][slot] = tRec
        P.wait("tensor", st4["tYe"][slot])
        tPV = None
        for tt in range(ntt):
            tPV = P.op("tensor", lambda t, tt=tt, h=h, slot=slot: t.matmul(
                BK(6 + slot, 0, DH), EB2[:, slot, ts(tt, 128)],
                V[:, tt, ds(h * DH, DH)], start=(tt == 0),
                stop=(tt == ntt - 1)), inc=(tt == ntt - 1))
        st4["tPV"][slot] = tPV

    def attn_tail(h, gi, slot):
        P.wait("scalar", st4["tPV"][slot])
        P.wait("scalar", st4["tRecDone"][slot])
        P.wait("scalar", st4["tY"][slot])
        tYe = P.op("scalar", lambda s, slot=slot: s.activation(
            YSB[:, slot, :], BK(6 + slot, 0, DH), AF.Identity,
            scale=reccol[:, slot:slot + 1]), inc=True)
        st4["tYe"][slot] = tYe
        P.wait("tensor", tYe)
        P.wait("tensor", bev4[5])
        tkT = P.op("tensor", lambda t, slot=slot: t.matmul(
            BK(5, 0, 128)[0:64, :], YSB[:, slot, :], ident[:, :],
            start=True, stop=True), inc=True)
        st4["tY"][slot] = tkT
        P.wait("vector", tkT)
        r0 = (h % 2) * 64
        gl = gi % 4
        tkE = P.op("vector", lambda v, r0=r0, h=h, gl=gl: v.tensor_copy(
            YTB[r0:r0 + 64, h // 2, ts(gl, 128)], BK(5, 0, 128)[0:64, :]),
            inc=True)
        bev4[5] = tkE
        return tkE

    prev_store = None
    for qg in range(4):
        P.wait("sync", prev_store)
        t_xt = P.dma("sync", XT4[:], xT_d.rearrange(
            "(c p) t -> p c t", p=128)[:, :, ds(qg * 512, 512)], sem="dx")
        prev = None
        pend = []
        last_yt = None
        for h in range(NH):
            for gl in range(4):
                gi = qg * 4 + gl
                slot = (h * 4 + gl) % 2
                stA = attn_frontA(h, gi, slot)
                if len(pend) == 2:
                    last_yt = attn_tail(*pend.pop(0))
                if prev is not None:
                    attn_mid(*prev)
                fr = attn_frontB(h, gi, slot, stA)
                prev = (h, gi, slot, fr)
                pend.append((h, gi, slot))
        attn_mid(*prev)
        last_yt = attn_tail(*pend.pop(0))
        last_yt = attn_tail(*pend.pop(0))
        P.wait("tensor", last_yt)
        P.wait("tensor", st4["tEhalf"][0])
        P.wait("tensor", st4["tEhalf"][1])
        P.wait("tensor", st4["tE"])
        P.wait("tensor", st4["tRecDone"][0])
        P.wait("tensor", st4["tRecDone"][1])
        P.wait("tensor", st4["tYe"][0])
        P.wait("tensor", st4["tYe"][1])
        P.wait("vector", st4["tYe"][0])
        P.wait("vector", st4["tYe"][1])
        st4["tE"] = st4["tMx"] = st4["tTr"] = None
        st4["tY"] = {0: None, 1: None}
        st4["tYe"] = {0: None, 1: None}
        st4["tPV"] = {0: None, 1: None}
        st4["tRecDone"] = {0: None, 1: None}
        prev_tX1 = None
        prev_stat = None
        bank_guard = {0: None, 1: None}
        pend_stats = None

        def emit_stats(ps):
            nonlocal prev_stat
            sfc, ssl, tBF_, tSQ_ = ps
            P.wait("tensor", tBF_)
            P.wait("tensor", tSQ_)
            P.op("tensor", lambda t, ssl=ssl: t.matmul(
                BK(2), ones[:, :], X1BF[:, ssl, :], start=(sfc == 0),
                stop=(sfc == EC - 1)))
            prev_stat = P.op("tensor", lambda t, ssl=ssl: t.matmul(
                BK(3), ones[:, :], X1SQ[:, ssl, :], start=(sfc == 0),
                stop=(sfc == EC - 1)), inc=True)

        for fc in range(EC):
            bki = fc % 2
            P.wait("tensor", bank_guard[bki])
            tWo = None
            for j in range(EC):
                tWo = P.op("tensor", lambda t, fc=fc, j=j, bki=bki: t.matmul(
                    BK(bki), WOB[:, j, ts(fc, 128)], YTB[:, j, :],
                    start=(j == 0), stop=(j == EC - 1)), inc=(j == EC - 1))
            if pend_stats is not None:
                emit_stats(pend_stats)
                pend_stats = None
            P.wait("scalar", tWo)
            P.wait("scalar", prev_tX1)
            sslot = fc % 2
            tS_ = P.op("scalar", lambda s, fc=fc, bki=bki, sslot=sslot:
                       s.activation(SCH[:, sslot, :], BK(bki), AF.Identity,
                                    bias=bL0[:, 3, fc:fc + 1]), inc=True)
            bank_guard[bki] = tS_
            P.wait("vector", tS_)
            P.wait("vector", t_xt)
            tGm = P.op("vector", lambda v, fc=fc, sslot=sslot, qg=qg:
                       v.tensor_mul(SCH[:, sslot, :], SCH[:, sslot, :],
                                    G[:, fc, ds(qg * 512, 512)]), inc=True)
            P.wait_self("vector", tGm)
            tX1 = P.op("vector", lambda v, fc=fc, sslot=sslot: v.tensor_add(
                XT4[:, fc, :], XT4[:, fc, :], SCH[:, sslot, :]), inc=True)
            prev_tX1 = tX1
            P.wait("sync", tX1)
            P.dma("sync", x1park_d.rearrange("(c p) t -> p c t", p=128)[
                :, fc, ds(qg * 512, 512)], XT4[:, fc, :])
            P.wait("vector", prev_stat)
            P.wait_self("vector", tX1)
            tBF = P.op("vector", lambda v, fc=fc, sslot=sslot: v.tensor_copy(
                X1BF[:, sslot, :], XT4[:, fc, :]), inc=True)
            P.wait("scalar", tX1)
            tSQ = P.op("scalar", lambda s, fc=fc, sslot=sslot: s.activation(
                X1SQ[:, sslot, :], XT4[:, fc, :], AF.Square), inc=True)
            pend_stats = (fc, sslot, tBF, tSQ)
        emit_stats(pend_stats)
        pend_stats = None
        P.wait("vector", prev_stat)
        P.wait("vector", tS_)
        tc1 = P.op("vector", lambda v: v.tensor_copy(SCH[:, 0, :], BK(2)),
                   inc=True)
        tc2 = P.op("vector", lambda v: v.tensor_copy(SCH[:, 1, :], BK(3)),
                   inc=True)
        P.wait("sync", tc2)
        P.dma("sync", statspark_d[0, :, ds(qg * 512, 512)], SCH[:, 0, :])
        t_sp = P.dma("sync", statspark_d[1, :, ds(qg * 512, 512)], SCH[:, 1, :])
        P.wait("tensor", tc2)
        P.wait("scalar", t_sp)
        P.wait("tensor", tS_)
        prev_store = t_sp

    tp = tap("x1", [D, T])
    if tp is not None:
        P.barrier()
        for c in range(EC):
            t_ = P.dma("sync", X1CH[:, 0, :], x1park_d.rearrange(
                "(c p) t -> p c t", p=128)[:, c, :], sem="dx2")
            P.wait("sync", t_)
            P.dma("sync", tp.rearrange("(c p) t -> p c t", p=128)[:, c, :],
                  X1CH[:, 0, :])
        P.barrier()

    P.barrier()
    P.muted = P.muted or (upto < 5)

    t_s0 = P.dma("sync", MB[:, :], statspark_d[0], sem="dx")
    t_s1 = P.dma("sync", VARB[:, :], statspark_d[1], sem="dx")
    P.wait("vector", t_s1)
    tk_mb = P.op("vector", lambda v: v.tensor_scalar_mul(
        MB[:, :], MB[:, :], 1.0 / D), inc=True)
    P.wait("scalar", tk_mb)
    t_sqm = P.op("scalar", lambda s: s.activation(SQM[:, :], MB[:, :],
                                                  AF.Square), inc=True)
    P.wait("vector", t_sqm)
    P.wait("vector", t_s1)
    tk_var = None
    for n in range(4):
        tk_var = P.op("vector", lambda v, n=n: v.scalar_tensor_tensor(
            VARB[:, ts(n, 512)], VARB[:, ts(n, 512)], 1.0 / D,
            SQM[:, ts(n, 512)], ALU.mult, ALU.subtract), inc=(n == 3))
    P.wait("scalar", tk_var)
    tLn1 = P.op("scalar", lambda s: s.activation(
        VARB[:, :], VARB[:, :], AF.Ln, bias=EPS), inc=True)
    P.wait_self("scalar", tLn1)
    t_rstd = P.op("scalar", lambda s: s.activation(RSTD[:, :], VARB[:, :],
                                                   AF.Exp, scale=-0.5),
                  inc=True)
    P.wait("vector", t_rstd)
    consumed = [None] * EC
    for c in range(EC):
        sl = c % 4
        if c >= 4:
            P.wait("sync", consumed[c - 4])
        t_l = P.dma("sync", X1CH[:, sl, :], x1park_d.rearrange(
            "(c p) t -> p c t", p=128)[:, c, :], sem=f"dr{sl}")
        P.wait("vector", t_l)
        tSub = P.op("vector", lambda v, c=c, sl=sl: v.tensor_sub(
            X1CH[:, sl, :], X1CH[:, sl, :], MB[:, :]), inc=True)
        P.wait_self("vector", tSub)
        tkn = P.op("vector", lambda v, c=c, sl=sl: v.tensor_mul(
            XN1[:, c, :], X1CH[:, sl, :], RSTD[:, :]), inc=True)
        consumed[c] = tkn
    tp = tap("xn1", [D, T])
    if tp is not None:
        dump_fm(tp, XN1, via=PRJ5)
    P.barrier()

    pm_state = {"last": None}
    tIW = None
    for k in range(8):
        tIW = P.op("scalar", lambda s, k=k: s.activation(
            identw8[:, k, :], ident[:, :], AF.Identity,
            scale=w8col[:, k:k + 1]), inc=(k == 7))

    def premix_gen():
        pl_guard = {0: None, 1: None, 2: None, 3: None}
        prev_ev = None
        dm_toks = {}

        def issue(gu):
            li = gu % 4
            P.wait("sync", pl_guard[li])
            dm_toks[gu] = P.dma("sync", PLAND32[:, li, :],
                                Wsel_d[gu % 8, (gu // 8) * 128:
                                       (gu // 8 + 1) * 128, :],
                                sem=f"dp{li}")
        for gu0 in range(4):
            issue(gu0)
        for oc in range(EC):
            tT = None
            for k in range(8):
                gu = oc * 8 + k
                li = gu % 4
                P.wait("tensor", dm_toks[gu])
                if oc == 0 and k == 0:
                    P.wait("tensor", tIW)
                P.wait("tensor", prev_ev)
                for u in range(8):
                    bki = 6 + u // 4
                    jj = u % 4
                    tT = P.op("tensor", lambda t, u=u, jj=jj, k=k, bki=bki,
                              li=li: t.matmul(
                        BK(bki, jj * 128, (jj + 1) * 128),
                        PLAND32[:, li, ts(u, 128)], identw8[:, k, :],
                        start=(k == 0 and u % 4 == 0), stop=(k == 7),
                        skip_group_check=True), inc=(u == 7))
                pl_guard[li] = tT
                if gu + 4 < 64:
                    issue(gu + 4)
                yield
            P.wait("vector", tT)
            tEv = P.op("vector", lambda v, oc=oc: v.tensor_copy(
                WMIXT[:, 0:8, ts(oc, 128)],
                PS[:, 3072:4096].rearrange("p (a b) -> p a b", a=8)),
                inc=True)
            prev_ev = tEv
            pm_state["last"] = tEv
            yield

    def emit_proj_L1(W_d_, bias_idx, dst_kind, src_xn, bg=None):
        t_w = P.dma("gpsimd", W1FULL[:],
                    W_d_.rearrange("(c p) f -> p c f", p=128), sem="dw")
        bank_prev = [None] * 4
        for fc in range(EC):
            if bg is not None:
                next(bg, None)
                next(bg, None)
            P.wait("tensor", t_w)
            tkMn = []
            for j in range(EC):
                for n in range(4):
                    if j == 0:
                        P.wait("tensor", bank_prev[n])
                    tk = P.op("tensor", lambda t, fc=fc, j=j, n=n: t.matmul(
                        BK(n), W1FULL[:, j, ts(fc, 128)],
                        src_xn[:, j, ts(n, 512)], start=(j == 0),
                        stop=(j == EC - 1)), inc=(j == EC - 1))
                    if j == EC - 1:
                        tkMn.append(tk)
            bias_ap = bL1[:, bias_idx, fc:fc + 1]
            if dst_kind in ("q1", "k1"):
                dstb = Q if dst_kind == "q1" else K
                scl = 2.0 if dst_kind == "q1" else 1.0
                if dst_kind == "q1":
                    bias_ap = bq2col[:, fc:fc + 1]
                for n in range(4):
                    P.wait("scalar", tkMn[n])
                    bank_prev[n] = P.op(
                        "scalar", lambda s, n=n, fc=fc, dstb=dstb,
                        bias_ap=bias_ap, scl=scl: s.activation(
                            dstb[:, fc, ts(n, 512)], BK(n), AF.Identity,
                            scale=scl, bias=bias_ap), inc=True)
                tkP = bank_prev[3]
            else:
                for n in range(4):
                    P.wait("scalar", tkMn[n])
                    bank_prev[n] = P.op(
                        "scalar", lambda s, n=n, bias_ap=bias_ap:
                        s.activation(VFM5[:, ts(n, 512)], BK(n),
                                     AF.Identity, bias=bias_ap), inc=True)
                tkP = bank_prev[3]
                P.wait("tensor", tkP)
                for bi, b0 in enumerate(range(0, TTILES, 4)):
                    bk = 4 + (bi % 2)
                    P.wait("tensor", g45[bk])
                    tkT = None
                    for u in range(4):
                        tkT = P.op("tensor", lambda t, b0=b0, u=u, bk=bk:
                                   t.matmul(BK(bk, u * 128, (u + 1) * 128),
                                            VFM5[:, ts(b0 + u, 128)],
                                            ident[:, :], start=True,
                                            stop=True), inc=(u == 3))
                    P.wait("vector", tkT)
                    g45[bk] = P.op("vector", lambda v, b0=b0, fc=fc, bk=bk:
                                   v.tensor_copy(V[:, b0:b0 + 4, ts(fc, 128)],
                                                 BK(bk, 0, 512).rearrange(
                                                     "p (a b) -> p a b", a=4)),
                                   inc=True)
        return t_w

    bg = premix_gen()
    emit_proj_L1(Wk1_d, 1, "k1", XN1, bg=bg)
    P.barrier()
    emit_proj_L1(Wv1_d, 2, "v1", XN1, bg=bg)
    for _ in bg:
        pass
    last_premix = pm_state["last"]
    P.barrier()
    t_bs = P.dma("sync", BS8[:], bsel_d, sem="dc")
    P.wait("vector", t_bs)
    tB1 = P.op("vector", lambda v: v.tensor_scalar(
        BS8B[:, :], BS8[:, :], w8col[0:8, 0:1], None, ALU.mult), inc=True)
    P.wait("tensor", tB1)
    tB2 = None
    for n in range(2):
        tB2 = P.op("tensor", lambda t, n=n: t.matmul(
            (BK(2, 0, 512) if n == 0 else BK(3, 0, 512))[0:1, :],
            onescol8[:, 0:1], BS8B[0:8, ts(n, 512)], start=True,
            stop=True), inc=(n == 1))
    P.wait("vector", tB2)
    tB3 = P.op("vector", lambda v: v.tensor_copy(
        PRJ5[0:1, 0:512], BK(2, 0, 512)[0:1, :]), inc=True)
    tB4 = P.op("vector", lambda v: v.tensor_copy(
        PRJ5[0:1, 512:1024], BK(3, 0, 512)[0:1, :]), inc=True)
    P.wait("sync", tB4)
    t_r = P.dma("sync", rowpark_d[0:1024], PRJ5[0:1, 0:1024])
    P.wait("sync", t_r)
    t_bm = P.dma("sync", bmixcol[:, :],
                 rowpark_d[0:1024].rearrange("(c p) -> p c", p=128),
                 sem="dx", allow_slow_non_contiguous=True)
    P.wait("vector", t_bm)
    P.wait("scalar", t_bm)
    P.wait("tensor", t_bm)
    P.barrier()

    bank_prev2 = [None] * 4
    for fc in range(EC):
        tkMn = []
        for j in range(EC):
            for n in range(4):
                if j == 0:
                    P.wait("tensor", bank_prev2[n])
                tk = P.op("tensor", lambda t, fc=fc, j=j, n=n: t.matmul(
                    BK(n), WMIXT[:, j, ts(fc, 128)], XN1[:, j, ts(n, 512)],
                    start=(j == 0), stop=(j == EC - 1)), inc=(j == EC - 1))
                if j == EC - 1:
                    tkMn.append(tk)
        for n in range(4):
            P.wait("scalar", tkMn[n])
            bank_prev2[n] = P.op("scalar", lambda s, n=n, fc=fc: s.activation(
                G[:, fc, ts(n, 512)], BK(n), AF.Silu,
                bias=bmixcol[:, fc:fc + 1]), inc=True)
    P.wait("tensor", bank_prev2[3])
    P.barrier()
    tbq2 = P.op("vector", lambda v: v.tensor_add(
        bq2col[:, :], bL1[:, 0, :], bL1[:, 0, :]), inc=True)
    P.wait("scalar", tbq2)
    emit_proj_L1(Wq1_d, 0, "q1", XN1)
    tp = tap("g2", [D, T])
    if tp is not None:
        dump_fm(tp, G, via=PRJ5)
    P.barrier()

    P.muted = P.muted or (upto < 6)
    tMsl = {0: None, 1: None}
    tMq = {0: None, 1: None}
    ETBf = ETB.rearrange("p s a b -> p s (a b)")
    tMk = tMqq = None
    for c in range(EC):
        sl = c % 2
        P.wait("scalar", tMsl[sl])
        tks = P.op("scalar", lambda s, c=c, sl=sl: s.activation(
            EB2[:, sl, :], K[:, c, :], AF.Square), inc=True)
        P.wait("vector", tMq[sl])
        tqs = P.op("vector", lambda v, c=c, sl=sl: v.tensor_mul(
            ETBf[:, sl, :], Q[:, c, :], Q[:, c, :]), inc=True)
        P.wait("tensor", tks)
        for n in range(4):
            tMk = P.op("tensor", lambda t, c=c, n=n, sl=sl: t.matmul(
                BK(n), negones[:, :], EB2[:, sl, ts(n, 512)],
                start=(c == 0), stop=(c == EC - 1)), inc=(n == 3))
        tMsl[sl] = tMk
        P.wait("tensor", tqs)
        for n in range(4):
            tMqq = P.op("tensor", lambda t, c=c, n=n, sl=sl: t.matmul(
                BK(4 + n), negones[:, :], ETBf[:, sl, ts(n, 512)],
                start=(c == 0), stop=(c == EC - 1)), inc=(n == 3))
        tMq[sl] = tMqq
    P.wait("vector", tMk)
    P.wait("vector", tMqq)
    tRw = None
    for n in range(4):
        tRw = P.op("vector", lambda v, n=n: v.tensor_scalar_mul(
            NKSQ_ROW[0:1, ts(n, 512)], BK(n)[0:1, :], 1.0), inc=(n == 3))
    tRw2 = None
    for n in range(4):
        tRw2 = P.op("vector", lambda v, n=n: v.tensor_scalar_mul(
            NQSQ_ROW[0:1, ts(n, 512)], BK(4 + n)[0:1, :], 0.25),
            inc=(n == 3))
    P.wait("tensor", tRw2)
    P.wait("scalar", tRw2)
    P.barrier()

    t_wo1 = P.dma("gpsimd", WOB[:],
                  Wo1_d.rearrange("(c p) f -> p c f", p=128), sem="dw")
    st7 = {"tS1": None, "tMx": None, "tTr": None, "tTrEvac": None,
           "tTrEvacV": None, "tTrEvacS": None,
           "tTr2": {0: None, 1: None}, "tY": None, "tRes": None}
    bev7 = {4: None, 5: None}

    def l1_frontA(gi, slot):
        P.wait("tensor", st7["tS1"])
        P.wait("scalar", st7["tRes"])
        tQK = None
        for n in range(4):
            for j in range(EC):
                P.op("tensor", lambda t, gi=gi, n=n, j=j: t.matmul(
                    BK(n), Q[:, j, ts(gi, 128)], K[:, j, ts(n, 512)],
                    start=(j == 0), stop=False))
            P.op("tensor", lambda t, gi=gi, n=n: t.matmul(
                BK(n), ones[0:1, :], NKSQ_ROW[0:1, ts(n, 512)],
                start=False, stop=False))
            tQK = P.op("tensor", lambda t, gi=gi, n=n: t.matmul(
                BK(n), NQSQ_ROW[0:1, ts(gi, 128)], onesrow[0:1, 0:512],
                start=False, stop=True), inc=(n == 3))
        P.wait("scalar", tQK)
        tSc = P.op("scalar", lambda s: s.activation(
            SSB[:, :], PS[:, 0:2048], AF.Exp), inc=True)
        st7["tS1"] = tSc
        return (tSc,)

    def l1_frontB(gi, slot, stA):
        (tSc,) = stA
        P.wait("scalar", st7["tTr2"][slot])
        P.wait_self("scalar", tSc)
        tE = P.op("scalar", lambda s, slot=slot: s.activation(
            EB2[:, slot, :], SSB[:, :], AF.Exp, bias=neg1col[:, 0:1],
            accum_out=sumcol[:, slot:slot + 1]), inc=True)
        P.wait("tensor", tE)
        tTr = None
        bparity = 0
        eng_flip = 0
        last_evac = None
        for b0 in range(0, TTILES, 4):
            bk5 = 4 + bparity
            P.wait("tensor", bev7[bk5])
            for u in range(4):
                tTr = P.op("tensor", lambda t, b0=b0, u=u, bk5=bk5, slot=slot:
                           t.matmul(BK(bk5, u * 128, (u + 1) * 128),
                                    EB2[:, slot, ts(b0 + u, 128)],
                                    ident[:, :], start=True, stop=True),
                           inc=(u == 3))
            eng = "vector" if eng_flip == 0 else "scalar"
            eng_flip ^= 1
            P.wait(eng, tTr)
            if eng == "vector":
                last_evac = P.op("vector", lambda v, b0=b0, slot=slot, bk5=bk5:
                                 v.tensor_copy(ETB[:, slot, b0:b0 + 4, :],
                                               BK(bk5, 0, 512)), inc=True)
            else:
                last_evac = P.op("scalar", lambda s, b0=b0, slot=slot, bk5=bk5:
                                 s.copy(ETB[:, slot, b0:b0 + 4, :],
                                        BK(bk5, 0, 512)), inc=True)
            bev7[bk5] = last_evac
            if eng == "vector":
                st7["tTrEvacV"] = last_evac
            else:
                st7["tTrEvacS"] = last_evac
            bparity ^= 1
        st7["tTrEvac"] = last_evac
        st7["tTr"] = tTr
        st7["tTr2"][slot] = tTr
        return (tE,)

    def l1_back(gi, slot, front):
        (tE,) = front
        P.wait("vector", tE)
        tRec = P.op("vector", lambda v, slot=slot: v.reciprocal(
            reccol[:, slot:slot + 1], sumcol[:, slot:slot + 1]), inc=True)
        P.wait("tensor", st7["tY"])
        P.wait("tensor", st7["tTrEvacV"])
        P.wait("tensor", st7["tTrEvacS"])
        tPV = None
        for half in range(2):
            for tt in range(TTILES):
                tPV = P.op("tensor", lambda t, tt=tt, half=half: t.matmul(
                    BK(6 + half), ETB[:, slot, tt, :],
                    V[:, tt, ds(half * 512, 512)], start=(tt == 0),
                    stop=(tt == TTILES - 1)),
                    inc=(half == 1 and tt == TTILES - 1))
        P.wait("scalar", tPV)
        P.wait("scalar", tRec)
        tY = None
        for half in range(2):
            tY = P.op("scalar", lambda s, half=half, slot=slot: s.activation(
                YSB1[:, slot, ds(half * 512, 512)], BK(6 + half),
                AF.Identity, scale=reccol[:, slot:slot + 1]), inc=(half == 1))
        st7["tY"] = tY
        P.wait("tensor", tY)
        gl = gi % 4
        tkE = None
        for bi, b0 in enumerate((0, 4)):
            bk5 = 4 + bi
            P.wait("tensor", bev7[bk5])
            tkT = None
            for u in range(4):
                tkT = P.op("tensor", lambda t, j=b0 + u, u=u, slot=slot,
                           bk5=bk5: t.matmul(
                    BK(bk5, u * 128, (u + 1) * 128),
                    YSB1[:, slot, ts(j, 128)], ident[:, :], start=True,
                    stop=True), inc=(u == 3))
            P.wait("vector", tkT)
            tkE = P.op("vector", lambda v, b0=b0, gl=gl, bk5=bk5:
                       v.tensor_copy(YTB[:, b0:b0 + 4, ts(gl, 128)],
                                     BK(bk5, 0, 512).rearrange(
                                         "p (a b) -> p a b", a=4)), inc=True)
            bev7[bk5] = tkE
        return tkE

    prev_store7 = None
    for qg in range(n7q):
        P.wait("sync", prev_store7)
        t_xt = P.dma("sync", XT7[:], x1park_d.rearrange(
            "(c p) t -> p c t", p=128)[:, :, ds(qg * 512, 512)], sem="dx")
        prev = None
        last_yt = None
        for gl in range(n7t):
            gi = qg * 4 + gl
            slot = gi % 2
            stA = l1_frontA(gi, slot)
            if prev is not None:
                last_yt = l1_back(*prev)
            fr = l1_frontB(gi, slot, stA)
            prev = (gi, slot, fr)
        last_yt = l1_back(*prev)
        P.wait("tensor", last_yt)
        P.wait("tensor", st7["tS1"])
        P.wait("tensor", t_wo1)
        st7["tS1"] = st7["tMx"] = st7["tTr"] = st7["tY"] = None
        prev_tX2 = None
        bank_guard7 = {0: None, 1: None}
        for fc in range(EC):
            bki = fc % 2
            P.wait("tensor", bank_guard7[bki])
            tWo = None
            for j in range(EC):
                tWo = P.op("tensor", lambda t, fc=fc, j=j, bki=bki: t.matmul(
                    BK(bki), WOB[:, j, ts(fc, 128)], YTB[:, j, :],
                    start=(j == 0), stop=(j == EC - 1)), inc=(j == EC - 1))
            P.wait("scalar", tWo)
            P.wait("scalar", prev_tX2)
            sslot = fc % 2
            tS_ = P.op("scalar", lambda s, fc=fc, bki=bki, sslot=sslot:
                       s.activation(SCH7[:, sslot, :], BK(bki), AF.Identity,
                                    bias=bL1[:, 3, fc:fc + 1]), inc=True)
            bank_guard7[bki] = tS_
            P.wait("vector", tS_)
            P.wait("vector", t_xt)
            tGm7 = P.op("vector", lambda v, fc=fc, sslot=sslot, qg=qg:
                        v.tensor_mul(SCH7[:, sslot, :], SCH7[:, sslot, :],
                                     G[:, fc, ds(qg * 512, 512)]), inc=True)
            P.wait_self("vector", tGm7)
            tX2 = P.op("vector", lambda v, fc=fc, sslot=sslot: v.tensor_add(
                XT7[:, fc, :], XT7[:, fc, :], SCH7[:, sslot, :]), inc=True)
            prev_tX2 = tX2
            P.wait("sync", tX2)
            prev_store7 = P.dma(
                "sync", out_d.rearrange("(c p) t -> p c t", p=128)[
                    :, fc, ds(qg * 512, 512)], XT7[:, fc, :])
        P.wait("tensor", tS_)
        st7["tRes"] = tX2

    for dsem in Prog.DMA_SEMS:
        fin = P.sems[dsem][1]
        if fin:
            h = P.sems[dsem][0]
            P.queues["sync"].append(lambda e, h=h, fin=fin: e.wait_ge(h, fin))

    P.replay()
    return nc, sorted(tap_d.keys())


def stage_inputs(inputs):
    x = np.asarray(inputs["L0_attn_x"])
    ids = np.asarray(inputs["L1_ffn_ids"]).astype(np.int64)
    Wsel = np.ascontiguousarray(np.asarray(inputs["L1_ffn_W"])[ids])
    bsel = np.ascontiguousarray(np.asarray(inputs["L1_ffn_b"])[ids])
    w8 = np.ascontiguousarray(
        np.asarray(inputs["L1_ffn_weights"]).reshape(1, 8))

    def bias_pack(*names):
        out = np.empty((128, 4, EC), dtype=np.float32)
        for i, nm in enumerate(names):
            b = np.asarray(inputs[nm], dtype=np.float32).reshape(EC, 128)
            out[:, i, :] = b.T
        return np.ascontiguousarray(out)

    shared = {
        "Wq0": np.ascontiguousarray(inputs["L0_attn_Wq"], dtype=np.float32),
        "Wk0": np.ascontiguousarray(inputs["L0_attn_Wk"], dtype=np.float32),
        "Wv0": np.ascontiguousarray(inputs["L0_attn_Wv"], dtype=np.float32),
        "Wo0": np.ascontiguousarray(inputs["L0_attn_Wo"], dtype=np.float32),
        "bL0": bias_pack("L0_attn_bq", "L0_attn_bk", "L0_attn_bv",
                         "L0_attn_bo"),
        "wg": np.ascontiguousarray(inputs["L0_ffn_w_gate"], dtype=np.int32),
        "wu": np.ascontiguousarray(inputs["L0_ffn_w_up"], dtype=np.int32),
        "wd": np.ascontiguousarray(inputs["L0_ffn_w_down"], dtype=np.int32),
        "Wq1": np.ascontiguousarray(inputs["L1_attn_Wq"], dtype=np.float32),
        "Wk1": np.ascontiguousarray(inputs["L1_attn_Wk"], dtype=np.float32),
        "Wv1": np.ascontiguousarray(inputs["L1_attn_Wv"], dtype=np.float32),
        "Wo1": np.ascontiguousarray(inputs["L1_attn_Wo"], dtype=np.float32),
        "bL1": bias_pack("L1_attn_bq", "L1_attn_bk", "L1_attn_bv",
                         "L1_attn_bo"),
        "Wsel": Wsel.astype(np.float32),
        "bsel": bsel.astype(np.float32),
        "w8": w8.astype(np.float32),
    }
    in_maps = []
    for c in range(8):
        m = dict(shared)
        m["xT"] = np.ascontiguousarray(
            np.asarray(x[c], dtype=np.float32).T)
        in_maps.append(m)
    return in_maps


_NC_CACHE = {}


def kernel(**inputs):
    if "main" not in _NC_CACHE:
        _NC_CACHE["main"] = build()
    nc, _ = _NC_CACHE["main"]
    in_maps = stage_inputs(inputs)
    res = run_bass_kernel_spmd(nc, in_maps, core_ids=list(range(8)))
    out = np.stack([np.asarray(r["outT"]).T for r in res.results], axis=0)
    return np.ascontiguousarray(out.astype(np.float32))


# version 35
# speedup vs baseline: 1.0182x; 1.0182x over previous
import numpy as np
import ml_dtypes
import concourse.bass as bass
import concourse.mybir as mybir
from concourse.bass_utils import run_bass_kernel_spmd

F32 = mybir.dt.float32
BF16 = mybir.dt.bfloat16
I32 = mybir.dt.int32
AF = mybir.ActivationFunctionType
ALU = mybir.AluOpType
AX = mybir.AxisListType.X
ts = bass.ts
ds = bass.ds

D = 1024
T = 2048
EC = 8
TTILES = 16
NH = 16
DH = 64
HID = 3072
HCH = 24
HG = 3
NG = HCH // HG
INV127 = 1.0 / 127.0
EPS = 1e-5

ENGS = ("sync", "gpsimd", "tensor", "vector", "scalar")


class Prog:

    DMA_SEMS = ("dc", "dx", "dx2", "dst", "dl0", "dl1", "dl2", "dl3",
                "dr0", "dr1", "dr2", "dr3", "dw", "dp0", "dp1", "dp2",
                "dp3")

    def __init__(self, nc):
        self.nc = nc
        self.queues = {e: [] for e in ENGS}
        self.muted = False
        self.sems = {}
        self.marks = {}
        for e in ENGS:
            self.sems[e] = [nc.alloc_semaphore(f"sem_{e}"), 0]
        for e in self.DMA_SEMS:
            self.sems[e] = [nc.alloc_semaphore(f"sem_{e}"), 0]

    def op(self, eng, fn, inc=False):
        if self.muted:
            return None
        if inc:
            s = self.sems[eng]
            s[1] += 1
            val, h = s[1], s[0]
            self.queues[eng].append(lambda e: fn(e).then_inc(h, 1))
            return (eng, val)
        self.queues[eng].append(fn)
        return None

    def wait_self(self, eng, tok):
        if tok is None or self.muted:
            return
        sem_name, val = tok
        h = self.sems[sem_name][0]
        self.queues[eng].append(lambda e: e.wait_ge(h, val))

    def wait(self, eng, tok):
        if tok is None or self.muted:
            return
        sem_name, val = tok
        if sem_name == eng:
            return
        key = (eng, sem_name)
        if self.marks.get(key, 0) >= val:
            return
        self.marks[key] = val
        h = self.sems[sem_name][0]
        self.queues[eng].append(lambda e: e.wait_ge(h, val))

    def dma(self, queue, out, in_, sem="dst", **kw):
        if self.muted:
            return None
        sem_name = sem
        s = self.sems[sem_name]
        s[1] += 16
        val, h = s[1], s[0]
        self.queues[queue].append(
            lambda e: e.dma_start(out=out, in_=in_, **kw).then_inc(h, 16))
        return (sem_name, val)

    @staticmethod
    def _nop(en):
        en.drain()
        return en.nop()

    def barrier(self):
        if self.muted:
            return
        dma_toks = [(d, self.sems[d][1]) for d in self.DMA_SEMS]
        for e in ENGS:
            for t in dma_toks:
                if t[1] > 0:
                    self.wait(e, t)
        toks = []
        for e in ENGS:
            s = self.sems[e]
            s[1] += 1
            h, val = s[0], s[1]
            self.queues[e].append(
                lambda en, h=h: Prog._nop(en).then_inc(h, 1))
            toks.append((e, val))
        for e in ENGS:
            for t in toks:
                self.wait(e, t)

    def self_sync(self, eng):
        if self.muted:
            return
        s = self.sems[eng]
        s[1] += 1
        h, val = s[0], s[1]
        self.queues[eng].append(lambda e: Prog._nop(e).then_inc(h, 1))
        self.queues[eng].append(lambda e: e.wait_ge(h, val))

    def replay(self):
        nc = self.nc
        with nc.Block() as block:
            @block.sync
            def _(e):
                for fn in self.queues["sync"]:
                    fn(e)

            @block.gpsimd
            def _(e):
                for fn in self.queues["gpsimd"]:
                    fn(e)

            @block.tensor
            def _(e):
                for fn in self.queues["tensor"]:
                    fn(e)

            @block.vector
            def _(e):
                for fn in self.queues["vector"]:
                    fn(e)

            @block.scalar
            def _(e):
                for fn in self.queues["scalar"]:
                    fn(e)


class Arena:
    def __init__(self, nc, size_bytes):
        self.nc = nc
        self.handle = nc.alloc_sbuf_tensor("arena", [128, size_bytes],
                                           mybir.dt.uint8)
        self.base = nc.lookup_mloc(self.handle).addr
        self.size = size_bytes

    def place(self, name, shape, dtype, off):
        nbytes = int(np.prod(shape[1:])) * mybir.dt.size(dtype)
        assert off + nbytes <= self.size, (name, off, nbytes)
        assert off % 32 == 0, (name, off)
        return self.nc.alloc_sbuf_tensor_at(name, list(shape), dtype,
                                            offset=self.base + off)


def make_constants():
    c = {}
    c["ident_bf"] = np.eye(128, dtype=ml_dtypes.bfloat16)
    c["ones_bf"] = np.ones((128, 128), dtype=ml_dtypes.bfloat16)
    c["negones_bf"] = np.full((128, 128), -1.0, dtype=ml_dtypes.bfloat16)
    c["onescol_f32"] = np.ones((1, 128), dtype=np.float32)
    c["onescol8_bf"] = np.ones((8, 1), dtype=ml_dtypes.bfloat16)
    c["onesrow_bf"] = np.ones((1, 512), dtype=ml_dtypes.bfloat16)
    c["tri_bf"] = np.tril(np.ones((128, 128))).astype(ml_dtypes.bfloat16)
    trineg = np.where(np.tril(np.ones((128, 128), dtype=bool)), 0.0, -1e30)
    c["trineg_bf"] = trineg.astype(ml_dtypes.bfloat16)
    c["trinegT_bf"] = trineg.T.astype(ml_dtypes.bfloat16)
    c["neg1col"] = np.full((128, 1), -1.0, dtype=np.float32)
    o64 = np.zeros((128, 2), dtype=ml_dtypes.bfloat16)
    o64[0:64, 0] = 1.0
    o64[64:128, 1] = 1.0
    c["ones64_bf"] = o64
    j = np.arange(T, dtype=np.float32)[:, None]
    i = np.arange(DH // 2, dtype=np.float32)[None, :]
    expo = (-2.0 * i / np.float32(DH)).astype(np.float32)
    theta = j * np.power(np.float32(10000.0), expo)
    th64 = theta.astype(np.float64)
    m = (np.cos(th64) + np.sin(th64)).astype(np.float32)
    mfull = np.concatenate([m, m], axis=1)
    Mk = np.empty((128, T), dtype=np.float32)
    for r in range(128):
        Mk[r, :] = mfull[:, r % 64]
    c["Mk"] = Mk
    c["Mq"] = (Mk * np.float32(0.125)).astype(np.float32)
    c["epscol"] = np.full((128, 1), EPS, dtype=np.float32)
    return c


def build(taps=(), upto=99, n7q=4, n7t=4):
    nc = bass.Bass("TRN2", target_bir_lowering=False)
    taps = set(taps)

    xT_d = nc.dram_tensor("xT", [D, T], F32, kind="ExternalInput").ap()
    Wq0_d = nc.dram_tensor("Wq0", [D, D], F32, kind="ExternalInput").ap()
    Wk0_d = nc.dram_tensor("Wk0", [D, D], F32, kind="ExternalInput").ap()
    Wv0_d = nc.dram_tensor("Wv0", [D, D], F32, kind="ExternalInput").ap()
    Wo0_d = nc.dram_tensor("Wo0", [D, D], F32, kind="ExternalInput").ap()
    bL0_d = nc.dram_tensor("bL0", [128, 4, EC], F32, kind="ExternalInput").ap()
    wg_d = nc.dram_tensor("wg", [HID, D], I32, kind="ExternalInput").ap()
    wu_d = nc.dram_tensor("wu", [HID, D], I32, kind="ExternalInput").ap()
    wd_d = nc.dram_tensor("wd", [D, HID], I32, kind="ExternalInput").ap()
    Wq1_d = nc.dram_tensor("Wq1", [D, D], F32, kind="ExternalInput").ap()
    Wk1_d = nc.dram_tensor("Wk1", [D, D], F32, kind="ExternalInput").ap()
    Wv1_d = nc.dram_tensor("Wv1", [D, D], F32, kind="ExternalInput").ap()
    Wo1_d = nc.dram_tensor("Wo1", [D, D], F32, kind="ExternalInput").ap()
    bL1_d = nc.dram_tensor("bL1", [128, 4, EC], F32, kind="ExternalInput").ap()
    Wsel_d = nc.dram_tensor("Wsel", [8, D, D], F32, kind="ExternalInput").ap()
    bsel_d = nc.dram_tensor("bsel", [8, D], F32, kind="ExternalInput").ap()
    w8_d = nc.dram_tensor("w8", [1, 8], F32, kind="ExternalInput").ap()
    out_d = nc.dram_tensor("outT", [D, T], F32, kind="ExternalOutput").ap()
    x1park_d = nc.dram_tensor("x1park", [D, T], F32, kind="Internal").ap()
    statspark_d = nc.dram_tensor("statspark", [2, 128, T], F32, kind="Internal").ap()
    rowpark_d = nc.dram_tensor("rowpark", [T], F32, kind="Internal").ap()

    C = make_constants()
    consts_dram = {k: nc.inline_tensor(v, k + "_c").ap() for k, v in C.items()}

    tap_d = {}

    def tap(name, shape):
        if name in taps:
            tap_d[name] = nc.dram_tensor("tap_" + name, list(shape), F32,
                                         kind="ExternalOutput").ap()
            return tap_d[name]
        return None

    ident = nc.alloc_sbuf_tensor("ident", [128, 128], BF16)
    ones = nc.alloc_sbuf_tensor("ones", [128, 128], BF16)
    negones = nc.alloc_sbuf_tensor("negones", [128, 128], BF16)
    onescol = nc.alloc_sbuf_tensor("onescol", [1, 128], F32)
    onescol8 = nc.alloc_sbuf_tensor("onescol8", [8, 1], BF16)
    onesrow = nc.alloc_sbuf_tensor("onesrow", [1, 512], BF16)
    tri = nc.alloc_sbuf_tensor("tri", [128, 128], BF16)
    trineg = nc.alloc_sbuf_tensor("trineg", [128, 128], BF16)
    trinegT = nc.alloc_sbuf_tensor("trinegT", [128, 128], BF16)
    neg1col = nc.alloc_sbuf_tensor("neg1col", [128, 1], F32)
    negBh = nc.alloc_sbuf_tensor("negBh", [128, NH], F32)
    ones64 = nc.alloc_sbuf_tensor("ones64", [128, 2], BF16)
    hnorm = nc.alloc_sbuf_tensor("hnorm", [2, 2, EC], F32)
    epscol = nc.alloc_sbuf_tensor("epscol", [128, 1], F32)
    bL0 = nc.alloc_sbuf_tensor("bL0_sb", [128, 4, EC], F32)
    bL1 = nc.alloc_sbuf_tensor("bL1_sb", [128, 4, EC], F32)
    w8sb = nc.alloc_sbuf_tensor("w8sb", [1, 8], F32)
    w8col = nc.alloc_sbuf_tensor("w8col", [128, 8], F32)
    bmixcol = nc.alloc_sbuf_tensor("bmixcol", [128, EC], F32)
    negmax = nc.alloc_sbuf_tensor("negmax", [128, 2], F32)
    sumcol = nc.alloc_sbuf_tensor("sumcol", [128, 2], F32)
    reccol = nc.alloc_sbuf_tensor("reccol", [128, 2], F32)
    Sm = nc.alloc_sbuf_tensor("Sm", [128, 2, 8], F32)
    nqsq = nc.alloc_sbuf_tensor("nqsq", [128, TTILES], F32)
    bq2col = nc.alloc_sbuf_tensor("bq2col", [128, EC], F32)

    nc.const_aps.register(F32, EPS, epscol[:, 0:1])

    oQ, oK, oV, oG, oXN = 0, 32 << 10, 64 << 10, 96 << 10, 128 << 10
    oW = 160 << 10
    oMISC = 200 << 10
    AR = 204 * 1024
    ar = Arena(nc, AR)

    Q = ar.place("Q", [128, EC, T], BF16, oQ)
    K = ar.place("K", [128, EC, T], BF16, oK)
    V = ar.place("V", [128, TTILES, D], BF16, oV)
    G = ar.place("G", [128, EC, T], BF16, oG)
    XN = ar.place("XN", [128, EC, T], BF16, oXN)
    ISCR = ar.place("ISCR", [128, 4, D], I32, oXN)

    XF = ar.place("XF", [128, EC, T], F32, oQ)
    XBF = ar.place("XBF", [128, EC, T], BF16, oG)
    XSQ = ar.place("XSQ", [128, EC, T], BF16, oV)
    MB = ar.place("MB", [128, T], F32, oW)
    SQM = ar.place("SQM", [128, T], F32, oW + (8 << 10))
    VARB = ar.place("VARB", [128, T], F32, oW + (16 << 10))
    RSTD = ar.place("RSTD", [128, T], F32, oW + (24 << 10))

    WGT = ar.place("WGT", [128, HG, EC * 128], BF16, oW)
    WUT = ar.place("WUT", [128, HG, EC * 128], BF16, oW + (6 << 10))
    WDT = ar.place("WDT", [128, EC, HG * 128], BF16, oW + (12 << 10))
    WGT2 = ar.place("WGT2", [128, HG, EC * 128], BF16, oV)
    WUT2 = ar.place("WUT2", [128, HG, EC * 128], BF16, oV + (6 << 10))
    WDT2 = ar.place("WDT2", [128, EC, HG * 128], BF16, oV + (12 << 10))
    HT = ar.place("HT", [128, HG, T], BF16, oW + (18 << 10))
    HGB = ar.place("HGB", [128, T], BF16, oW + (30 << 10))
    LANDB = ar.place("LANDB", [128, 2, D], BF16, oW + (34 << 10))
    ACC = ar.place("ACC", [128, EC, T], F32, oQ)
    LAND = ar.place("LAND", [128, 2, D], BF16, oMISC)

    WBLK8 = ar.place("WBLK8", [128, 2, EC, 128], BF16, oW)
    PRJ = ar.place("PRJ", [128, T], F32, oW + (4 << 10))
    VFM = ar.place("VFM", [128, T], BF16, oW + (12 << 10))
    MQT = ar.place("MQT", [128, T], F32, oW + (16 << 10))
    MKT = ar.place("MKT", [128, T], F32, oW + (24 << 10))
    LANDC = ar.place("LANDC", [128, 2, D], BF16, oW + (32 << 10))

    WOB = ar.place("WOB", [128, EC, D], BF16, oW)
    EB2 = ar.place("EB2", [128, 2, T], BF16, oW + (16 << 10))
    ETB = ar.place("ETB", [128, 2, TTILES, 128], BF16, oW + (24 << 10))
    YTB = ar.place("YTB", [128, EC, 512], BF16, oW + (32 << 10))
    YSB = ar.place("YSB", [128, 2, DH], BF16, oMISC)
    YSB1 = ar.place("YSB1", [128, 2, D], BF16, oMISC)
    XT4 = ar.place("XT4", [128, EC, 512], F32, oXN)
    SCH = ar.place("SCH", [128, 2, 512], F32, oXN + (16 << 10))
    X1BF = ar.place("X1BF", [128, 2, 512], BF16, oXN + (20 << 10))
    X1SQ = ar.place("X1SQ", [128, 2, 512], BF16, oXN + (22 << 10))

    XN1 = ar.place("XN1", [128, EC, T], BF16, oXN)
    X1CH = ar.place("X1CH", [128, 4, T], F32, oQ)
    WMIXT = ar.place("WMIXT", [128, EC, D], BF16, oW)
    PLAND32 = ar.place("PLAND32", [128, 4, D], F32, oG)
    identw8 = ar.place("identw8", [128, 8, 128], F32, oG + (16 << 10))
    BS8 = ar.place("BS8", [8, D], F32, oW + (16 << 10))
    BS8B = ar.place("BS8B", [8, D], BF16, oW + (20 << 10))
    W1FULL = ar.place("W1FULL", [128, EC, D], BF16, oW + (16 << 10))
    PRJ5 = ar.place("PRJ5", [128, T], F32, oW + (32 << 10))
    VFM5 = ar.place("VFM5", [128, T], BF16, oMISC)
    PLAND = ar.place("PLAND", [128, 2, D], BF16, oQ)

    NQSQ_ROW = ar.place("NQSQ_ROW", [1, T], BF16, oXN)
    NKSQ_ROW = ar.place("NKSQ_ROW", [1, T], BF16, oXN + (4 << 10))
    SSB = ar.place("SSB", [128, T], F32, oXN + (8 << 10))
    SCH7 = ar.place("SCH7", [128, 2, 512], F32, oXN + (8 << 10))
    XT7 = ar.place("XT7", [128, EC, 512], F32, oXN + (16 << 10))

    PS = nc.alloc_psum_tensor("PS", [128, 4096], F32)

    def BK(i, lo=0, hi=512):
        return PS[:, i * 512 + lo:i * 512 + hi]

    P = Prog(nc)

    for dst, key in ((ident, "ident_bf"), (ones, "ones_bf"),
                     (negones, "negones_bf"), (onescol, "onescol_f32"),
                     (onescol8, "onescol8_bf"), (onesrow, "onesrow_bf"),
                     (tri, "tri_bf"),
                     (trineg, "trineg_bf"), (trinegT, "trinegT_bf"),
                     (neg1col, "neg1col"), (ones64, "ones64_bf"),
                     (epscol, "epscol")):
        P.dma("sync", dst[:], consts_dram[key], sem="dc")
    P.dma("sync", bL0[:], bL0_d, sem="dc")
    P.dma("sync", bL1[:], bL1_d, sem="dc")
    t_w8 = P.dma("sync", w8sb[:], w8_d, sem="dc")
    pl_dma = []
    for pi in range(4):
        wsrc_ = wg_d if pi < 3 else wu_d
        hc0 = pi if pi < 3 else 0
        pl_dma.append(P.dma("sync", ISCR[:, pi, :],
                            wsrc_[hc0 * 128:(hc0 + 1) * 128, :],
                            sem=f"dp{pi}"))
    preland = []
    for pi in range(4):
        P.wait("vector", pl_dma[pi])
        preland.append(P.op("vector", lambda v, pi=pi: v.tensor_copy(
            (LAND[:, pi, :] if pi < 2 else LANDB[:, pi - 2, :]),
            ISCR[:, pi, :]), inc=True))
    t_x_chunks = []
    for c in range(EC):
        if c >= 4:
            P.wait("sync", t_x_chunks[c - 4])
        t_x_chunks.append(P.dma(
            "sync", XF[:, c, :],
            xT_d.rearrange("(c p) t -> p c t", p=128)[:, c, :],
            sem=f"dr{c % 4}"))
    t_x = t_x_chunks[-1]
    P.wait("tensor", t_w8)
    tk = P.op("tensor", lambda t: t.matmul(BK(7, 0, 8), onescol[0:1, :],
                                           w8sb[0:1, :], start=True,
                                           stop=True), inc=True)
    P.wait("vector", tk)
    t_w8c = P.op("vector", lambda v: v.tensor_copy(w8col[:], BK(7, 0, 8)),
                 inc=True)
    P.wait("tensor", t_w8c)

    def emit_layernorm(in_f32, xbf, xsq, xn_out, dep_toks):
        for c in range(EC):
            tk_ = dep_toks[c] if len(dep_toks) == EC else dep_toks[-1]
            P.wait("vector", tk_)
            cp = P.op("vector", lambda v, c=c: v.tensor_copy(
                xbf[:, c, :], in_f32[:, c, :]), inc=True)
            P.wait("scalar", tk_)
            sq = P.op("scalar", lambda s, c=c: s.activation(
                xsq[:, c, :], in_f32[:, c, :], AF.Square), inc=True)
            P.wait("tensor", cp)
            P.wait("tensor", sq)
            for n in range(4):
                P.op("tensor", lambda t, c=c, n=n: t.matmul(
                    BK(n), ones[:, :], xbf[:, c, ts(n, 512)],
                    start=(c == 0), stop=(c == EC - 1)),
                    inc=(c == EC - 1 and n == 3))
            for n in range(4):
                P.op("tensor", lambda t, c=c, n=n: t.matmul(
                    BK(4 + n), ones[:, :], xsq[:, c, ts(n, 512)],
                    start=(c == 0), stop=(c == EC - 1)),
                    inc=(c == EC - 1 and n == 3))
        tk_sq = ("tensor", P.sems["tensor"][1])
        tk_sum = ("tensor", P.sems["tensor"][1] - 1)
        P.wait("vector", tk_sum)
        tk_mb = None
        for n in range(4):
            tk_mb = P.op("vector", lambda v, n=n: v.tensor_scalar_mul(
                MB[:, ts(n, 512)], BK(n), 1.0 / D), inc=(n == 3))
        P.wait("scalar", tk_mb)
        t_sqm = P.op("scalar", lambda s: s.activation(
            SQM[:, :], MB[:, :], AF.Square), inc=True)
        P.wait("vector", tk_sq)
        P.wait("vector", t_sqm)
        tk_var = None
        for n in range(4):
            tk_var = P.op("vector", lambda v, n=n: v.scalar_tensor_tensor(
                VARB[:, ts(n, 512)], BK(4 + n), 1.0 / D,
                SQM[:, ts(n, 512)], ALU.mult, ALU.subtract), inc=(n == 3))
        P.wait("scalar", tk_var)
        tLn0 = P.op("scalar", lambda s: s.activation(
            VARB[:, :], VARB[:, :], AF.Ln, bias=EPS), inc=True)
        P.wait_self("scalar", tLn0)
        t_rstd = P.op("scalar", lambda s: s.activation(
            RSTD[:, :], VARB[:, :], AF.Exp, scale=-0.5), inc=True)
        P.wait("vector", t_rstd)
        sub_toks = {}
        tok = None
        for c in range(EC):
            sub_toks[c] = P.op("vector", lambda v, c=c: v.tensor_sub(
                in_f32[:, c, :], in_f32[:, c, :], MB[:, :]), inc=True)
            if c >= 1:
                P.wait_self("vector", sub_toks[c - 1])
                tok = P.op("vector", lambda v, c=c - 1: v.tensor_mul(
                    xn_out[:, c, :], in_f32[:, c, :], RSTD[:, :]), inc=True)
        P.wait_self("vector", sub_toks[EC - 1])
        tok = P.op("vector", lambda v: v.tensor_mul(
            xn_out[:, EC - 1, :], in_f32[:, EC - 1, :], RSTD[:, :]),
            inc=True)
        return tok, tk_var

    def dump_fm(tp_ap, src, nchunks=EC, via=None, cvt=True):
        P.barrier()
        for c in range(nchunks):
            if cvt:
                tkc = P.op("vector", lambda v, c=c: v.tensor_copy(
                    via[:, :], src[:, c, :]), inc=True)
                P.wait("sync", tkc)
                t_ = P.dma("sync",
                           tp_ap.rearrange("(c p) t -> p c t", p=128)[:, c, :],
                           via[:, :])
                P.wait("vector", t_)
            else:
                t_ = P.dma("sync",
                           tp_ap.rearrange("(c p) t -> p c t", p=128)[:, c, :],
                           src[:, c, :])
        P.barrier()

    tk_xn, tk_var0 = emit_layernorm(XF, XBF, XSQ, XN, t_x_chunks)
    tp = tap("xn", [D, T])
    if tp is not None:
        dump_fm(tp, XN, via=PRJ)

    P.muted = P.muted or (upto < 2)

    prev_scalar_read = None
    land_guard2 = {0: None, 1: None, 2: None, 3: None}

    def LANDX(li):
        return LAND[:, li, :] if li < 2 else LANDB[:, li - 2, :]

    bev67 = {6: None, 7: None}
    p2flip = [0]

    evac_force = [None]

    def p2_evac(dst_ap_fn, bk6, ln):
        if evac_force[0] is not None:
            eng = evac_force[0]
        else:
            eng = "vector" if p2flip[0] == 0 else "scalar"
            p2flip[0] ^= 1
        if eng == "vector":
            tok = P.op("vector", lambda v: v.tensor_copy(
                dst_ap_fn(), BK(bk6, 0, ln)), inc=True)
        else:
            tok = P.op("scalar", lambda s: s.copy(
                dst_ap_fn(), BK(bk6, 0, ln)), inc=True)
        bev67[bk6] = tok
        return tok
    tr_evac_guard = None
    ht_tok = None
    WB = [(WGT, WUT, WDT), (WGT2, WUT2, WDT2)]
    prep_done = {}
    ht_cur = [None]

    def p2_slices(g):
        out = []
        for w_i in range(2):
            wsrc = (wg_d, wu_d)[w_i]
            for hl in range(HG):
                hc = g * HG + hl
                out.append(("gu", w_i, hl,
                            lambda li, wsrc=wsrc, hc=hc: P.dma(
                                "gpsimd", LANDX(li),
                                wsrc[hc * 128:(hc + 1) * 128, :],
                                sem=f"dl{li}")))
        for dc in range(EC):
            out.append(("wd", 0, dc,
                        lambda li, dc=dc, g=g: P.dma(
                            "gpsimd", LANDX(li)[:, 0:HG * 128],
                            wd_d[dc * 128:(dc + 1) * 128,
                                 g * HG * 128:(g + 1) * HG * 128],
                            sem=f"dl{li}")))
        return out

    def prep_issue4(g):
        toks = []
        sl = p2_slices(g)
        for i in range(4):
            li = i % 4
            P.wait("gpsimd", land_guard2[li])
            toks.append(sl[i][3](li))
        return toks

    def prep_gen(g, pre_toks):
        WGTg, WUTg, WDTg = WB[(g + 1) % 2]
        lastv = lasts = None
        sl = p2_slices(g)
        toks = list(pre_toks)
        tparity = 0
        first = True
        for i, (kind, w_i, idx, dma_fn) in enumerate(sl):
            li = i % 4
            P.wait("tensor", toks[i])
            if first:
                P.wait("tensor", ht_cur[0])
                first = False
            if kind == "gu":
                dstbuf = WGTg if w_i == 0 else WUTg
                for b0 in (0, 4):
                    bk6 = 6 + tparity
                    P.wait("tensor", bev67[bk6])
                    tkT = None
                    for u in range(4):
                        tkT = P.op("tensor", lambda t, li=li, j=b0 + u, u=u,
                                   bk6=bk6: t.matmul(
                            BK(bk6, u * 128, (u + 1) * 128),
                            LANDX(li)[:, ts(j, 128)], ident[:, :],
                            start=True, stop=True), inc=(u == 3))
                    land_guard2[li] = tkT
                    eng = evac_force[0] or (
                        "vector" if p2flip[0] == 0 else "scalar")
                    P.wait(eng, tkT)
                    tok = p2_evac(
                        lambda dstbuf=dstbuf, hl=idx, b0=b0:
                        dstbuf[:, hl, b0 * 128:(b0 + 4) * 128], bk6, 512)
                    if eng == "vector":
                        lastv = tok
                    else:
                        lasts = tok
                    tparity ^= 1
            else:
                bk6 = 6 + tparity
                P.wait("tensor", bev67[bk6])
                tkT = None
                for hl in range(HG):
                    tkT = P.op("tensor", lambda t, li=li, hl=hl, bk6=bk6:
                               t.matmul(
                        BK(bk6, hl * 128, (hl + 1) * 128),
                        LANDX(li)[:, ts(hl, 128)], ident[:, :], start=True,
                        stop=True), inc=(hl == HG - 1))
                land_guard2[li] = tkT
                eng = evac_force[0] or (
                    "vector" if p2flip[0] == 0 else "scalar")
                P.wait(eng, tkT)
                tok = p2_evac(lambda dc=idx, WDTg=WDTg:
                              WDTg[:, dc, 0:HG * 128], bk6, HG * 128)
                if eng == "vector":
                    lastv = tok
                else:
                    lasts = tok
                tparity ^= 1
            prep_done[g] = (lastv, lasts)
            if i + 4 < len(sl):
                li4 = (i + 4) % 4
                P.wait("gpsimd", land_guard2[li4])
                toks.append(sl[i + 4][3](li4))
            yield

    ht_cur[0] = tk_var0
    evac_force[0] = "scalar"
    for _ in prep_gen(0, preland):
        pass
    evac_force[0] = None
    prev_evac = [None, None]
    for g in range(NG):
        WGTg, WUTg, WDTg = WB[(g + 1) % 2]
        if g == 0:
            P.wait("tensor", tk_xn)
        P.wait("tensor", prep_done[g][0])
        P.wait("tensor", prep_done[g][1])
        P.wait("tensor", prev_evac[0])
        P.wait("tensor", prev_evac[1])
        pre4 = prep_issue4(g + 1) if g + 1 < NG else None
        ht_tok = None
        for hl in range(HG):
            P.wait("tensor", prev_scalar_read)
            for j in range(EC):
                for n in range(4):
                    P.op("tensor", lambda t, hl=hl, j=j, n=n, WGTg=WGTg:
                         t.matmul(
                        BK(n), WGTg[:, hl, ts(j, 128)], XN[:, j, ts(n, 512)],
                        start=(j == 0), stop=(j == EC - 1)),
                        inc=(j == EC - 1 and n == 3))
            tg = ("tensor", P.sems["tensor"][1])
            P.wait("tensor", ht_tok)
            for j in range(EC):
                for n in range(4):
                    P.op("tensor", lambda t, hl=hl, j=j, n=n, WUTg=WUTg:
                         t.matmul(
                        BK(4 + n), WUTg[:, hl, ts(j, 128)],
                        XN[:, j, ts(n, 512)], start=(j == 0),
                        stop=(j == EC - 1)), inc=(j == EC - 1 and n == 3))
            tu = ("tensor", P.sems["tensor"][1])
            P.wait("scalar", tg)
            P.wait("scalar", ht_tok)
            tkg = None
            for n in range(4):
                tkg = P.op("scalar", lambda s, n=n: s.activation(
                    HGB[:, ts(n, 512)], BK(n), AF.Silu, scale=INV127),
                    inc=(n == 3))
            P.wait("vector", tkg)
            P.wait("vector", tu)
            ht_tok = None
            for n in range(4):
                ht_tok = P.op("vector", lambda v, hl=hl, n=n:
                              v.scalar_tensor_tensor(
                                  HT[:, hl, ts(n, 512)], BK(4 + n), INV127,
                                  HGB[:, ts(n, 512)], ALU.mult, ALU.mult),
                              inc=True)
            prev_scalar_read = tkg
        ht_cur[0] = ht_tok
        P.wait("tensor", ht_tok)
        gen = prep_gen(g + 1, pre4) if g + 1 < NG else None
        evac_toks = [None, None]
        for ui, (dc, n) in enumerate([(dc, n) for dc in range(EC)
                                      for n in range(4)]):
            bki = ui % 2
            P.wait("tensor", evac_toks[bki])
            tkd = None
            for hl in range(HG):
                tkd = P.op("tensor", lambda t, dc=dc, n=n, hl=hl, bki=bki,
                           WDTg=WDTg:
                           t.matmul(BK(bki), WDTg[:, dc, ts(hl, 128)],
                                    HT[:, hl, ts(n, 512)],
                                    start=(hl == 0), stop=(hl == HG - 1)),
                           inc=(hl == HG - 1))
            P.wait("vector", tkd)
            if g == 0:
                tke = P.op("vector", lambda v, dc=dc, n=n, bki=bki:
                           v.tensor_copy(ACC[:, dc, ts(n, 512)], BK(bki)),
                           inc=True)
            else:
                tke = P.op("vector", lambda v, dc=dc, n=n, bki=bki:
                           v.tensor_add(ACC[:, dc, ts(n, 512)],
                                        ACC[:, dc, ts(n, 512)], BK(bki)),
                           inc=True)
            evac_toks[bki] = tke
            last_acc_tok = tke
            if gen is not None and ui % 2 == 1:
                try:
                    next(gen)
                except StopIteration:
                    gen = None
        while gen is not None:
            try:
                next(gen)
            except StopIteration:
                gen = None
        prev_evac = list(evac_toks)
    P.wait("scalar", last_acc_tok)
    tkgG = None
    for dc in range(EC):
        tkgG = P.op("scalar", lambda s, dc=dc: s.activation(
            G[:, dc, :], ACC[:, dc, :], AF.Silu, scale=INV127),
            inc=(dc == EC - 1))
    tp = tap("g", [D, T])
    if tp is not None:
        P.wait("vector", tkgG)
        dump_fm(tp, G, via=PRJ)

    P.barrier()
    P.muted = P.muted or (upto < 3)

    t_mq = P.dma("sync", MQT[:, :], consts_dram["Mq"], sem="dc")
    t_mk = P.dma("sync", MKT[:, :], consts_dram["Mk"], sem="dc")

    land_guard = {0: None, 1: None, 2: None, 3: None}
    g45 = {4: None, 5: None}

    def emit_proj_L0(W_d, bias_idx, kind):
        outs = {"q": Q, "k": K}
        wblk_evac = [None, None]
        wblk_read = [None, None]

        def LANDY(li):
            return LAND[:, li, :] if li < 2 else LANDC[:, li - 2, :]

        def stage_T(fc):
            li = fc % 4
            sl = fc % 2
            P.wait("gpsimd", land_guard[li])
            t_land = P.dma("gpsimd", LANDY(li),
                           W_d[fc * 128:(fc + 1) * 128, :], sem=f"dl{li}")
            P.wait("tensor", t_land)
            P.wait("tensor", wblk_read[sl])
            for bi, b0 in enumerate((0, 4)):
                bk = 4 + bi
                P.wait("tensor", g45[bk])
                tkT = None
                for u in range(4):
                    tkT = P.op("tensor", lambda t, li=li, j=b0 + u, u=u,
                               bk=bk: t.matmul(
                        BK(bk, u * 128, (u + 1) * 128), LANDY(li)[:, ts(j, 128)],
                        ident[:, :], start=True, stop=True), inc=(u == 3))
                land_guard[li] = tkT
                P.wait("vector", tkT)
                g45[bk] = P.op("vector", lambda v, sl=sl, b0=b0, bk=bk:
                               v.tensor_copy(WBLK8[:, sl, b0:b0 + 4, :],
                                             BK(bk, 0, 512)), inc=True)
            wblk_evac[sl] = g45[5]

        def stage_M(fc):
            sl = fc % 2
            P.wait("tensor", wblk_evac[sl])
            tkM = None
            for j in range(EC):
                for n in range(4):
                    tkM = P.op("tensor", lambda t, sl=sl, j=j, n=n: t.matmul(
                        BK(n), WBLK8[:, sl, j, :], XN[:, j, ts(n, 512)],
                        start=(j == 0), stop=(j == EC - 1)),
                        inc=(j == EC - 1 and n == 3))
            wblk_read[sl] = tkM
            P.wait("scalar", tkM)
            tkP = None
            for n in range(4):
                tkP = P.op("scalar", lambda s, n=n, fc=fc: s.activation(
                    PRJ[:, ts(n, 512)], BK(n), AF.Identity,
                    bias=bL0[:, bias_idx, fc:fc + 1]), inc=(n == 3))
            P.wait("vector", tkP)
            if kind in ("q", "k"):
                P.wait("vector", t_mk)
                tbl = MQT if kind == "q" else MKT
                tkR = P.op("vector", lambda v, fc=fc, tbl=tbl: v.tensor_mul(
                    outs[kind][:, fc, :], PRJ[:, :], tbl[:, :]), inc=True)
                P.wait("tensor", tkR)
                P.wait("scalar", tkR)
            else:
                tkC = P.op("vector", lambda v: v.tensor_copy(
                    VFM[:, :], PRJ[:, :]), inc=True)
                P.wait("tensor", tkC)
                P.wait("scalar", tkC)
                for bi, b0 in enumerate(range(0, TTILES, 4)):
                    bk = 4 + (bi % 2)
                    P.wait("tensor", g45[bk])
                    tkT = None
                    for u in range(4):
                        tkT = P.op("tensor", lambda t, b0=b0, u=u, bk=bk:
                                   t.matmul(BK(bk, u * 128, (u + 1) * 128),
                                            VFM[:, ts(b0 + u, 128)],
                                            ident[:, :], start=True,
                                            stop=True), inc=(u == 3))
                    P.wait("vector", tkT)
                    g45[bk] = P.op("vector", lambda v, b0=b0, fc=fc, bk=bk:
                                   v.tensor_copy(V[:, b0:b0 + 4, ts(fc, 128)],
                                                 BK(bk, 0, 512).rearrange(
                                                     "p (a b) -> p a b", a=4)),
                                   inc=True)

        stage_T(0)
        for fc in range(EC):
            if fc + 1 < EC:
                stage_T(fc + 1)
            stage_M(fc)

    emit_proj_L0(Wk0_d, 1, "k")
    emit_proj_L0(Wv0_d, 2, "v")
    emit_proj_L0(Wq0_d, 0, "q")
    for nm, buf in (("q0", Q), ("k0", K)):
        tp = tap(nm, [D, T])
        if tp is not None:
            dump_fm(tp, buf, via=PRJ)
    tp = tap("v0", [T, D])
    if tp is not None:
        dump_fm(tp, V, nchunks=TTILES, via=PRJ[:, 0:D].rearrange("p t -> p t"))
    P.barrier()
    P.muted = P.muted or (upto < 4)

    for fc in range(EC):
        li = fc % 2
        P.wait("gpsimd", land_guard[li])
        t_land = P.dma("gpsimd", LAND[:, li, :],
                       Wo0_d[fc * 128:(fc + 1) * 128, :], sem=f"dl{li}")
        P.wait("tensor", t_land)
        for bi, b0 in enumerate((0, 4)):
            bk = 4 + bi
            P.wait("tensor", g45[bk])
            tkT = None
            for u in range(4):
                tkT = P.op("tensor", lambda t, li=li, j=b0 + u, u=u, bk=bk:
                           t.matmul(BK(bk, u * 128, (u + 1) * 128),
                                    LAND[:, li, ts(j, 128)], ident[:, :],
                                    start=True, stop=True), inc=(u == 3))
            land_guard[li] = tkT
            P.wait("vector", tkT)
            g45[bk] = P.op("vector", lambda v, b0=b0, fc=fc, bk=bk:
                           v.tensor_copy(WOB[:, b0:b0 + 4, ts(fc, 128)],
                                         BK(bk, 0, 512).rearrange(
                                             "p (a b) -> p a b", a=4)),
                           inc=True)
    P.barrier()

    tSqP = {0: None, 1: None}
    tRed = {0: None, 1: None}
    tEBg = None
    for i, (si, c) in enumerate([(si, c) for si in range(2)
                                 for c in range(EC)]):
        srcb = (Q, K)[si]
        sl = i % 2
        P.wait("scalar", tSqP[sl])
        tSq = P.op("scalar", lambda s, c=c, srcb=srcb, sl=sl: s.activation(
            EB2[:, sl, :], srcb[:, c, :], AF.Square), inc=True)
        P.wait("tensor", tSq)
        P.wait("tensor", tRed[sl])
        tmm = None
        for n in range(4):
            tmm = P.op("tensor", lambda t, n=n, sl=sl: t.matmul(
                BK(4 * sl + n)[0:2, :], ones64[:, :], EB2[:, sl, ts(n, 512)],
                start=True, stop=True), inc=(n == 3))
        tSqP[sl] = tmm
        P.wait("vector", tmm)
        tEBg = P.op("vector", lambda v, si=si, c=c, sl=sl: v.reduce_max(
            hnorm[0:2, si, c:c + 1],
            PS[0:2, sl * 2048:sl * 2048 + 2048], axis=AX), inc=True)
        tRed[sl] = tEBg
    P.wait("vector", tRed[0])
    P.wait("vector", tRed[1])
    P.wait_self("vector", tEBg)
    tPr = P.op("vector", lambda v: v.tensor_mul(
        hnorm[0:2, 0, :], hnorm[0:2, 0, :], hnorm[0:2, 1, :]), inc=True)
    P.wait("scalar", tPr)
    tLn = P.op("scalar", lambda s: s.activation(
        hnorm[0:2, 0, :], hnorm[0:2, 0, :], AF.Ln), inc=True)
    P.wait_self("scalar", tLn)
    tEx = P.op("scalar", lambda s: s.activation(
        hnorm[0:2, 1, :], hnorm[0:2, 0, :], AF.Exp, scale=0.5), inc=True)
    P.wait("vector", tEx)
    tNg = P.op("vector", lambda v: v.tensor_scalar_mul(
        hnorm[0:2, 1, :], hnorm[0:2, 1, :], -1.0), inc=True)
    P.wait("sync", tNg)
    t_r = P.dma("sync", rowpark_d[0:16], hnorm[0:2, 1, :])
    P.wait("sync", t_r)
    t_rb = P.dma("sync", SCH[0:1, 0, 0:16],
                 rowpark_d[0:16].rearrange("(p c) -> c p", p=2), sem="dx2",
                 allow_slow_non_contiguous=True)
    P.wait("tensor", t_rb)
    tBc = P.op("tensor", lambda t: t.matmul(
        BK(0, 0, NH), onescol[0:1, :], SCH[0:1, 0, 0:16], start=True,
        stop=True), inc=True)
    P.wait("vector", tBc)
    tBv = P.op("vector", lambda v: v.tensor_copy(negBh[:, :], BK(0, 0, NH)),
               inc=True)
    P.wait("tensor", tBv)
    P.wait("scalar", tBv)

    def head_q(h, gi):
        r0 = (h % 2) * 64
        return Q[r0:r0 + 64, h // 2, ts(gi, 128)]

    def head_k(h, sl):
        r0 = (h % 2) * 64
        return K[r0:r0 + 64, h // 2, sl]

    st4 = {"tE": None, "tMx": None, "tTr": None, "tTrEvac": None,
           "tTrEvacV": None, "tTrEvacS": None, "tEslot": {0: None, 1: None},
           "tTr2": {0: None, 1: None}, "tY": {0: None, 1: None},
           "tYe": {0: None, 1: None}, "tPV": {0: None, 1: None},
           "tRecDone": {0: None, 1: None}}
    bev4 = {4: None, 5: None}

    st4["tSums"] = {0: None, 1: None}
    st4["tEhalf"] = {0: None, 1: None}

    def attn_frontA(h, gi, slot):
        klen = (gi + 1) * 128
        ntt = klen // 128
        off = slot * 1024 if klen <= 1024 else 0
        if klen <= 1024:
            P.wait("tensor", st4["tEhalf"][slot])
        else:
            P.wait("tensor", st4["tEhalf"][0])
            P.wait("tensor", st4["tEhalf"][1])
        tS = None
        for tt in range(ntt):
            tS = P.op("tensor", lambda t, h=h, gi=gi, tt=tt, off=off: t.matmul(
                PS[:, off + tt * 128:off + (tt + 1) * 128],
                head_k(h, ts(tt, 128)), head_q(h, gi), start=True,
                stop=True), inc=(tt == ntt - 1))
        P.wait("vector", tS)
        tMask = P.op("vector", lambda v, klen=klen, off=off: v.tensor_add(
            PS[:, off + klen - 128:off + klen],
            PS[:, off + klen - 128:off + klen], trinegT[:, :]), inc=True)
        return (klen, ntt, off, tMask)

    def attn_frontB(h, gi, slot, stA):
        klen, ntt, off, tMask = stA
        P.wait("scalar", tMask)
        P.wait("scalar", st4["tTr2"][slot])
        if klen <= 1024:
            tE = P.op("scalar", lambda s, klen=klen, slot=slot, h=h, off=off:
                      s.activation(EB2[:, slot, 0:klen],
                                   PS[:, off:off + klen], AF.Exp,
                                   bias=negBh[:, h:h + 1]), inc=True)
            st4["tEhalf"][slot] = tE
        else:
            tE = P.op("scalar", lambda s, klen=klen, slot=slot, h=h:
                      s.activation(EB2[:, slot, 0:klen], PS[:, 0:klen],
                                   AF.Exp, bias=negBh[:, h:h + 1]), inc=True)
            st4["tEhalf"][0] = tE
            st4["tEhalf"][1] = tE
        st4["tE"] = tE
        st4["tTr2"][slot] = tE
        return (ntt, tE)

    def attn_mid(h, gi, slot, front):
        (ntt, tE) = front
        P.wait("tensor", tE)
        P.wait("tensor", st4["tRecDone"][0])
        P.wait("tensor", st4["tRecDone"][1])
        tSum = None
        for tt in range(ntt):
            tSum = P.op("tensor", lambda t, tt=tt, slot=slot: t.matmul(
                BK(4, slot, slot + 1), EB2[:, slot, ts(tt, 128)],
                ones[:, 0:1], start=(tt == 0), stop=(tt == ntt - 1),
                skip_group_check=True), inc=(tt == ntt - 1))
        st4["tSums"][slot] = tSum
        P.wait("vector", tSum)
        P.wait("vector", st4["tYe"][slot])
        tRec = P.op("vector", lambda v, slot=slot: v.reciprocal(
            reccol[:, slot:slot + 1], BK(4, slot, slot + 1)), inc=True)
        st4["tRecDone"][slot] = tRec
        P.wait("tensor", st4["tYe"][slot])
        tPV = None
        for tt in range(ntt):
            tPV = P.op("tensor", lambda t, tt=tt, h=h, slot=slot: t.matmul(
                BK(6 + slot, 0, DH), EB2[:, slot, ts(tt, 128)],
                V[:, tt, ds(h * DH, DH)], start=(tt == 0),
                stop=(tt == ntt - 1)), inc=(tt == ntt - 1))
        st4["tPV"][slot] = tPV

    def attn_tail(h, gi, slot):
        P.wait("scalar", st4["tPV"][slot])
        P.wait("scalar", st4["tRecDone"][slot])
        P.wait("scalar", st4["tY"][slot])
        tYe = P.op("scalar", lambda s, slot=slot: s.activation(
            YSB[:, slot, :], BK(6 + slot, 0, DH), AF.Identity,
            scale=reccol[:, slot:slot + 1]), inc=True)
        st4["tYe"][slot] = tYe
        P.wait("tensor", tYe)
        P.wait("tensor", bev4[5])
        tkT = P.op("tensor", lambda t, slot=slot: t.matmul(
            BK(5, 0, 128)[0:64, :], YSB[:, slot, :], ident[:, :],
            start=True, stop=True), inc=True)
        st4["tY"][slot] = tkT
        P.wait("vector", tkT)
        r0 = (h % 2) * 64
        gl = gi % 4
        tkE = P.op("vector", lambda v, r0=r0, h=h, gl=gl: v.tensor_copy(
            YTB[r0:r0 + 64, h // 2, ts(gl, 128)], BK(5, 0, 128)[0:64, :]),
            inc=True)
        bev4[5] = tkE
        return tkE

    prev_store = None
    for qg in range(4):
        P.wait("sync", prev_store)
        t_xt = P.dma("sync", XT4[:], xT_d.rearrange(
            "(c p) t -> p c t", p=128)[:, :, ds(qg * 512, 512)], sem="dx")
        prev = None
        pend = []
        last_yt = None
        for h in range(NH):
            for gl in range(4):
                gi = qg * 4 + gl
                slot = (h * 4 + gl) % 2
                stA = attn_frontA(h, gi, slot)
                if len(pend) == 2:
                    last_yt = attn_tail(*pend.pop(0))
                if prev is not None:
                    attn_mid(*prev)
                fr = attn_frontB(h, gi, slot, stA)
                prev = (h, gi, slot, fr)
                pend.append((h, gi, slot))
        attn_mid(*prev)
        last_yt = attn_tail(*pend.pop(0))
        last_yt = attn_tail(*pend.pop(0))
        P.wait("tensor", last_yt)
        P.wait("tensor", st4["tEhalf"][0])
        P.wait("tensor", st4["tEhalf"][1])
        P.wait("tensor", st4["tE"])
        P.wait("tensor", st4["tRecDone"][0])
        P.wait("tensor", st4["tRecDone"][1])
        P.wait("tensor", st4["tYe"][0])
        P.wait("tensor", st4["tYe"][1])
        P.wait("vector", st4["tYe"][0])
        P.wait("vector", st4["tYe"][1])
        st4["tE"] = st4["tMx"] = st4["tTr"] = None
        st4["tY"] = {0: None, 1: None}
        st4["tYe"] = {0: None, 1: None}
        st4["tPV"] = {0: None, 1: None}
        st4["tRecDone"] = {0: None, 1: None}
        prev_tX1 = None
        prev_stat = None
        bank_guard = {0: None, 1: None}
        pend_stats = None

        def emit_stats(ps):
            nonlocal prev_stat
            sfc, ssl, tBF_, tSQ_ = ps
            P.wait("tensor", tBF_)
            P.wait("tensor", tSQ_)
            P.op("tensor", lambda t, ssl=ssl: t.matmul(
                BK(2), ones[:, :], X1BF[:, ssl, :], start=(sfc == 0),
                stop=(sfc == EC - 1)))
            prev_stat = P.op("tensor", lambda t, ssl=ssl: t.matmul(
                BK(3), ones[:, :], X1SQ[:, ssl, :], start=(sfc == 0),
                stop=(sfc == EC - 1)), inc=True)

        for fc in range(EC):
            bki = fc % 2
            P.wait("tensor", bank_guard[bki])
            tWo = None
            for j in range(EC):
                tWo = P.op("tensor", lambda t, fc=fc, j=j, bki=bki: t.matmul(
                    BK(bki), WOB[:, j, ts(fc, 128)], YTB[:, j, :],
                    start=(j == 0), stop=(j == EC - 1)), inc=(j == EC - 1))
            if pend_stats is not None:
                emit_stats(pend_stats)
                pend_stats = None
            P.wait("scalar", tWo)
            P.wait("scalar", prev_tX1)
            sslot = fc % 2
            tS_ = P.op("scalar", lambda s, fc=fc, bki=bki, sslot=sslot:
                       s.activation(SCH[:, sslot, :], BK(bki), AF.Identity,
                                    bias=bL0[:, 3, fc:fc + 1]), inc=True)
            bank_guard[bki] = tS_
            P.wait("vector", tS_)
            P.wait("vector", t_xt)
            tGm = P.op("vector", lambda v, fc=fc, sslot=sslot, qg=qg:
                       v.tensor_mul(SCH[:, sslot, :], SCH[:, sslot, :],
                                    G[:, fc, ds(qg * 512, 512)]), inc=True)
            P.wait_self("vector", tGm)
            tX1 = P.op("vector", lambda v, fc=fc, sslot=sslot: v.tensor_add(
                XT4[:, fc, :], XT4[:, fc, :], SCH[:, sslot, :]), inc=True)
            prev_tX1 = tX1
            P.wait("sync", tX1)
            P.dma("sync", x1park_d.rearrange("(c p) t -> p c t", p=128)[
                :, fc, ds(qg * 512, 512)], XT4[:, fc, :])
            P.wait("vector", prev_stat)
            P.wait_self("vector", tX1)
            tBF = P.op("vector", lambda v, fc=fc, sslot=sslot: v.tensor_copy(
                X1BF[:, sslot, :], XT4[:, fc, :]), inc=True)
            P.wait("scalar", tX1)
            tSQ = P.op("scalar", lambda s, fc=fc, sslot=sslot: s.activation(
                X1SQ[:, sslot, :], XT4[:, fc, :], AF.Square), inc=True)
            pend_stats = (fc, sslot, tBF, tSQ)
        emit_stats(pend_stats)
        pend_stats = None
        P.wait("vector", prev_stat)
        P.wait("vector", tS_)
        tc1 = P.op("vector", lambda v: v.tensor_copy(SCH[:, 0, :], BK(2)),
                   inc=True)
        tc2 = P.op("vector", lambda v: v.tensor_copy(SCH[:, 1, :], BK(3)),
                   inc=True)
        P.wait("sync", tc2)
        P.dma("sync", statspark_d[0, :, ds(qg * 512, 512)], SCH[:, 0, :])
        t_sp = P.dma("sync", statspark_d[1, :, ds(qg * 512, 512)], SCH[:, 1, :])
        P.wait("tensor", tc2)
        P.wait("scalar", t_sp)
        P.wait("tensor", tS_)
        prev_store = t_sp

    tp = tap("x1", [D, T])
    if tp is not None:
        P.barrier()
        for c in range(EC):
            t_ = P.dma("sync", X1CH[:, 0, :], x1park_d.rearrange(
                "(c p) t -> p c t", p=128)[:, c, :], sem="dx2")
            P.wait("sync", t_)
            P.dma("sync", tp.rearrange("(c p) t -> p c t", p=128)[:, c, :],
                  X1CH[:, 0, :])
        P.barrier()

    P.barrier()
    P.muted = P.muted or (upto < 5)

    t_s0 = P.dma("sync", MB[:, :], statspark_d[0], sem="dx")
    t_s1 = P.dma("sync", VARB[:, :], statspark_d[1], sem="dx")
    P.wait("vector", t_s1)
    tk_mb = P.op("vector", lambda v: v.tensor_scalar_mul(
        MB[:, :], MB[:, :], 1.0 / D), inc=True)
    P.wait("scalar", tk_mb)
    t_sqm = P.op("scalar", lambda s: s.activation(SQM[:, :], MB[:, :],
                                                  AF.Square), inc=True)
    P.wait("vector", t_sqm)
    P.wait("vector", t_s1)
    tk_var = None
    for n in range(4):
        tk_var = P.op("vector", lambda v, n=n: v.scalar_tensor_tensor(
            VARB[:, ts(n, 512)], VARB[:, ts(n, 512)], 1.0 / D,
            SQM[:, ts(n, 512)], ALU.mult, ALU.subtract), inc=(n == 3))
    P.wait("scalar", tk_var)
    tLn1 = P.op("scalar", lambda s: s.activation(
        VARB[:, :], VARB[:, :], AF.Ln, bias=EPS), inc=True)
    P.wait_self("scalar", tLn1)
    t_rstd = P.op("scalar", lambda s: s.activation(RSTD[:, :], VARB[:, :],
                                                   AF.Exp, scale=-0.5),
                  inc=True)
    P.wait("vector", t_rstd)
    consumed = [None] * EC
    for c in range(EC):
        sl = c % 4
        if c >= 4:
            P.wait("sync", consumed[c - 4])
        t_l = P.dma("sync", X1CH[:, sl, :], x1park_d.rearrange(
            "(c p) t -> p c t", p=128)[:, c, :], sem=f"dr{sl}")
        P.wait("vector", t_l)
        tSub = P.op("vector", lambda v, c=c, sl=sl: v.tensor_sub(
            X1CH[:, sl, :], X1CH[:, sl, :], MB[:, :]), inc=True)
        P.wait_self("vector", tSub)
        tkn = P.op("vector", lambda v, c=c, sl=sl: v.tensor_mul(
            XN1[:, c, :], X1CH[:, sl, :], RSTD[:, :]), inc=True)
        consumed[c] = tkn
    tp = tap("xn1", [D, T])
    if tp is not None:
        dump_fm(tp, XN1, via=PRJ5)
    P.barrier()

    pm_state = {"last": None}
    tIW = None
    for k in range(8):
        tIW = P.op("scalar", lambda s, k=k: s.activation(
            identw8[:, k, :], ident[:, :], AF.Identity,
            scale=w8col[:, k:k + 1]), inc=(k == 7))

    def premix_gen():
        pl_guard = {0: None, 1: None, 2: None, 3: None}
        prev_ev = None
        dm_toks = {}

        def issue(gu):
            li = gu % 4
            P.wait("sync", pl_guard[li])
            dm_toks[gu] = P.dma("sync", PLAND32[:, li, :],
                                Wsel_d[gu % 8, (gu // 8) * 128:
                                       (gu // 8 + 1) * 128, :],
                                sem=f"dp{li}")
        for gu0 in range(4):
            issue(gu0)
        for oc in range(EC):
            tT = None
            for k in range(8):
                gu = oc * 8 + k
                li = gu % 4
                P.wait("tensor", dm_toks[gu])
                if oc == 0 and k == 0:
                    P.wait("tensor", tIW)
                P.wait("tensor", prev_ev)
                for u in range(8):
                    bki = 6 + u // 4
                    jj = u % 4
                    tT = P.op("tensor", lambda t, u=u, jj=jj, k=k, bki=bki,
                              li=li: t.matmul(
                        BK(bki, jj * 128, (jj + 1) * 128),
                        PLAND32[:, li, ts(u, 128)], identw8[:, k, :],
                        start=(k == 0 and u % 4 == 0), stop=(k == 7),
                        skip_group_check=True), inc=(u == 7))
                pl_guard[li] = tT
                if gu + 4 < 64:
                    issue(gu + 4)
                yield
            P.wait("vector", tT)
            tEv = P.op("vector", lambda v, oc=oc: v.tensor_copy(
                WMIXT[:, 0:8, ts(oc, 128)],
                PS[:, 3072:4096].rearrange("p (a b) -> p a b", a=8)),
                inc=True)
            prev_ev = tEv
            pm_state["last"] = tEv
            yield

    def emit_proj_L1(W_d_, bias_idx, dst_kind, src_xn, bg=None):
        t_w = P.dma("gpsimd", W1FULL[:],
                    W_d_.rearrange("(c p) f -> p c f", p=128), sem="dw")
        bank_prev = [None] * 4
        for fc in range(EC):
            if bg is not None:
                next(bg, None)
                next(bg, None)
            P.wait("tensor", t_w)
            tkMn = []
            for j in range(EC):
                for n in range(4):
                    if j == 0:
                        P.wait("tensor", bank_prev[n])
                    tk = P.op("tensor", lambda t, fc=fc, j=j, n=n: t.matmul(
                        BK(n), W1FULL[:, j, ts(fc, 128)],
                        src_xn[:, j, ts(n, 512)], start=(j == 0),
                        stop=(j == EC - 1)), inc=(j == EC - 1))
                    if j == EC - 1:
                        tkMn.append(tk)
            bias_ap = bL1[:, bias_idx, fc:fc + 1]
            if dst_kind in ("q1", "k1"):
                dstb = Q if dst_kind == "q1" else K
                scl = 2.0 if dst_kind == "q1" else 1.0
                if dst_kind == "q1":
                    bias_ap = bq2col[:, fc:fc + 1]
                for n in range(4):
                    P.wait("scalar", tkMn[n])
                    bank_prev[n] = P.op(
                        "scalar", lambda s, n=n, fc=fc, dstb=dstb,
                        bias_ap=bias_ap, scl=scl: s.activation(
                            dstb[:, fc, ts(n, 512)], BK(n), AF.Identity,
                            scale=scl, bias=bias_ap), inc=True)
                tkP = bank_prev[3]
            else:
                for n in range(4):
                    P.wait("scalar", tkMn[n])
                    bank_prev[n] = P.op(
                        "scalar", lambda s, n=n, bias_ap=bias_ap:
                        s.activation(VFM5[:, ts(n, 512)], BK(n),
                                     AF.Identity, bias=bias_ap), inc=True)
                tkP = bank_prev[3]
                P.wait("tensor", tkP)
                for bi, b0 in enumerate(range(0, TTILES, 4)):
                    bk = 4 + (bi % 2)
                    P.wait("tensor", g45[bk])
                    tkT = None
                    for u in range(4):
                        tkT = P.op("tensor", lambda t, b0=b0, u=u, bk=bk:
                                   t.matmul(BK(bk, u * 128, (u + 1) * 128),
                                            VFM5[:, ts(b0 + u, 128)],
                                            ident[:, :], start=True,
                                            stop=True), inc=(u == 3))
                    P.wait("vector", tkT)
                    g45[bk] = P.op("vector", lambda v, b0=b0, fc=fc, bk=bk:
                                   v.tensor_copy(V[:, b0:b0 + 4, ts(fc, 128)],
                                                 BK(bk, 0, 512).rearrange(
                                                     "p (a b) -> p a b", a=4)),
                                   inc=True)
        return t_w

    bg = premix_gen()
    emit_proj_L1(Wk1_d, 1, "k1", XN1, bg=bg)
    P.barrier()
    emit_proj_L1(Wv1_d, 2, "v1", XN1, bg=bg)
    for _ in bg:
        pass
    last_premix = pm_state["last"]
    P.barrier()
    t_bs = P.dma("sync", BS8[:], bsel_d, sem="dc")
    P.wait("vector", t_bs)
    tB1 = P.op("vector", lambda v: v.tensor_scalar(
        BS8B[:, :], BS8[:, :], w8col[0:8, 0:1], None, ALU.mult), inc=True)
    P.wait("tensor", tB1)
    tB2 = None
    for n in range(2):
        tB2 = P.op("tensor", lambda t, n=n: t.matmul(
            (BK(2, 0, 512) if n == 0 else BK(3, 0, 512))[0:1, :],
            onescol8[:, 0:1], BS8B[0:8, ts(n, 512)], start=True,
            stop=True), inc=(n == 1))
    P.wait("vector", tB2)
    tB3 = P.op("vector", lambda v: v.tensor_copy(
        PRJ5[0:1, 0:512], BK(2, 0, 512)[0:1, :]), inc=True)
    tB4 = P.op("vector", lambda v: v.tensor_copy(
        PRJ5[0:1, 512:1024], BK(3, 0, 512)[0:1, :]), inc=True)
    P.wait("sync", tB4)
    t_r = P.dma("sync", rowpark_d[0:1024], PRJ5[0:1, 0:1024])
    P.wait("sync", t_r)
    t_bm = P.dma("sync", bmixcol[:, :],
                 rowpark_d[0:1024].rearrange("(c p) -> p c", p=128),
                 sem="dx", allow_slow_non_contiguous=True)
    P.wait("vector", t_bm)
    P.wait("scalar", t_bm)
    P.wait("tensor", t_bm)
    P.barrier()

    bank_prev2 = [None] * 4
    for fc in range(EC):
        tkMn = []
        for j in range(EC):
            for n in range(4):
                if j == 0:
                    P.wait("tensor", bank_prev2[n])
                tk = P.op("tensor", lambda t, fc=fc, j=j, n=n: t.matmul(
                    BK(n), WMIXT[:, j, ts(fc, 128)], XN1[:, j, ts(n, 512)],
                    start=(j == 0), stop=(j == EC - 1)), inc=(j == EC - 1))
                if j == EC - 1:
                    tkMn.append(tk)
        for n in range(4):
            P.wait("scalar", tkMn[n])
            bank_prev2[n] = P.op("scalar", lambda s, n=n, fc=fc: s.activation(
                G[:, fc, ts(n, 512)], BK(n), AF.Silu,
                bias=bmixcol[:, fc:fc + 1]), inc=True)
    P.wait("tensor", bank_prev2[3])
    P.barrier()
    tbq2 = P.op("vector", lambda v: v.tensor_add(
        bq2col[:, :], bL1[:, 0, :], bL1[:, 0, :]), inc=True)
    P.wait("scalar", tbq2)
    emit_proj_L1(Wq1_d, 0, "q1", XN1)
    tp = tap("g2", [D, T])
    if tp is not None:
        dump_fm(tp, G, via=PRJ5)
    P.barrier()

    P.muted = P.muted or (upto < 6)
    tMsl = {0: None, 1: None}
    tMq = {0: None, 1: None}
    ETBf = ETB.rearrange("p s a b -> p s (a b)")
    tMk = tMqq = None
    for c in range(EC):
        sl = c % 2
        P.wait("scalar", tMsl[sl])
        tks = P.op("scalar", lambda s, c=c, sl=sl: s.activation(
            EB2[:, sl, :], K[:, c, :], AF.Square), inc=True)
        P.wait("vector", tMq[sl])
        tqs = P.op("vector", lambda v, c=c, sl=sl: v.tensor_mul(
            ETBf[:, sl, :], Q[:, c, :], Q[:, c, :]), inc=True)
        P.wait("tensor", tks)
        for n in range(4):
            tMk = P.op("tensor", lambda t, c=c, n=n, sl=sl: t.matmul(
                BK(n), negones[:, :], EB2[:, sl, ts(n, 512)],
                start=(c == 0), stop=(c == EC - 1)), inc=(n == 3))
        tMsl[sl] = tMk
        P.wait("tensor", tqs)
        for n in range(4):
            tMqq = P.op("tensor", lambda t, c=c, n=n, sl=sl: t.matmul(
                BK(4 + n), negones[:, :], ETBf[:, sl, ts(n, 512)],
                start=(c == 0), stop=(c == EC - 1)), inc=(n == 3))
        tMq[sl] = tMqq
    P.wait("vector", tMk)
    P.wait("vector", tMqq)
    tRw = None
    for n in range(4):
        tRw = P.op("vector", lambda v, n=n: v.tensor_scalar_mul(
            NKSQ_ROW[0:1, ts(n, 512)], BK(n)[0:1, :], 1.0), inc=(n == 3))
    tRw2 = None
    for n in range(4):
        tRw2 = P.op("vector", lambda v, n=n: v.tensor_scalar_mul(
            NQSQ_ROW[0:1, ts(n, 512)], BK(4 + n)[0:1, :], 0.25),
            inc=(n == 3))
    P.wait("tensor", tRw2)
    P.wait("scalar", tRw2)
    P.barrier()

    t_wo1 = P.dma("gpsimd", WOB[:],
                  Wo1_d.rearrange("(c p) f -> p c f", p=128), sem="dw")
    st7 = {"tS1": None, "tMx": None, "tTr": None, "tTrEvac": None,
           "tTrEvacV": None, "tTrEvacS": None,
           "tTr2": {0: None, 1: None}, "tY": None, "tRes": None}
    bev7 = {4: None, 5: None}

    def l1_frontA(gi, slot):
        P.wait("tensor", st7["tS1"])
        P.wait("scalar", st7["tRes"])
        tQK = None
        for n in range(4):
            for j in range(EC):
                P.op("tensor", lambda t, gi=gi, n=n, j=j: t.matmul(
                    BK(n), Q[:, j, ts(gi, 128)], K[:, j, ts(n, 512)],
                    start=(j == 0), stop=False))
            P.op("tensor", lambda t, gi=gi, n=n: t.matmul(
                BK(n), ones[0:1, :], NKSQ_ROW[0:1, ts(n, 512)],
                start=False, stop=False))
            tQK = P.op("tensor", lambda t, gi=gi, n=n: t.matmul(
                BK(n), NQSQ_ROW[0:1, ts(gi, 128)], onesrow[0:1, 0:512],
                start=False, stop=True), inc=(n == 1 or n == 3))
            if n == 1:
                tQK1 = tQK
        P.wait("scalar", tQK1)
        P.op("scalar", lambda s: s.activation(
            SSB[:, 0:1024], PS[:, 0:1024], AF.Exp))
        P.wait("scalar", tQK)
        tSc = P.op("scalar", lambda s: s.activation(
            SSB[:, 1024:2048], PS[:, 1024:2048], AF.Exp), inc=True)
        st7["tS1"] = tSc
        return (tSc,)

    def l1_frontB(gi, slot, stA):
        (tSc,) = stA
        P.wait("scalar", st7["tTr2"][slot])
        P.wait_self("scalar", tSc)
        tE = P.op("scalar", lambda s, slot=slot: s.activation(
            EB2[:, slot, :], SSB[:, :], AF.Exp, bias=neg1col[:, 0:1],
            accum_out=sumcol[:, slot:slot + 1]), inc=True)
        P.wait("tensor", tE)
        tTr = None
        bparity = 0
        eng_flip = 0
        last_evac = None
        for b0 in range(0, TTILES, 4):
            bk5 = 4 + bparity
            P.wait("tensor", bev7[bk5])
            for u in range(4):
                tTr = P.op("tensor", lambda t, b0=b0, u=u, bk5=bk5, slot=slot:
                           t.matmul(BK(bk5, u * 128, (u + 1) * 128),
                                    EB2[:, slot, ts(b0 + u, 128)],
                                    ident[:, :], start=True, stop=True),
                           inc=(u == 3))
            eng = "vector" if eng_flip == 0 else "scalar"
            eng_flip ^= 1
            P.wait(eng, tTr)
            if eng == "vector":
                last_evac = P.op("vector", lambda v, b0=b0, slot=slot, bk5=bk5:
                                 v.tensor_copy(ETB[:, slot, b0:b0 + 4, :],
                                               BK(bk5, 0, 512)), inc=True)
            else:
                last_evac = P.op("scalar", lambda s, b0=b0, slot=slot, bk5=bk5:
                                 s.copy(ETB[:, slot, b0:b0 + 4, :],
                                        BK(bk5, 0, 512)), inc=True)
            bev7[bk5] = last_evac
            if eng == "vector":
                st7["tTrEvacV"] = last_evac
            else:
                st7["tTrEvacS"] = last_evac
            bparity ^= 1
        st7["tTrEvac"] = last_evac
        st7["tTr"] = tTr
        st7["tTr2"][slot] = tTr
        return (tE,)

    def l1_back(gi, slot, front):
        (tE,) = front
        P.wait("vector", tE)
        tRec = P.op("vector", lambda v, slot=slot: v.reciprocal(
            reccol[:, slot:slot + 1], sumcol[:, slot:slot + 1]), inc=True)
        P.wait("tensor", st7["tY"])
        P.wait("tensor", st7["tTrEvacV"])
        P.wait("tensor", st7["tTrEvacS"])
        tPV = None
        for half in range(2):
            for tt in range(TTILES):
                tPV = P.op("tensor", lambda t, tt=tt, half=half: t.matmul(
                    BK(6 + half), ETB[:, slot, tt, :],
                    V[:, tt, ds(half * 512, 512)], start=(tt == 0),
                    stop=(tt == TTILES - 1)),
                    inc=(half == 1 and tt == TTILES - 1))
        P.wait("scalar", tPV)
        P.wait("scalar", tRec)
        tY = None
        for half in range(2):
            tY = P.op("scalar", lambda s, half=half, slot=slot: s.activation(
                YSB1[:, slot, ds(half * 512, 512)], BK(6 + half),
                AF.Identity, scale=reccol[:, slot:slot + 1]), inc=(half == 1))
        st7["tY"] = tY
        P.wait("tensor", tY)
        gl = gi % 4
        tkE = None
        for bi, b0 in enumerate((0, 4)):
            bk5 = 4 + bi
            P.wait("tensor", bev7[bk5])
            tkT = None
            for u in range(4):
                tkT = P.op("tensor", lambda t, j=b0 + u, u=u, slot=slot,
                           bk5=bk5: t.matmul(
                    BK(bk5, u * 128, (u + 1) * 128),
                    YSB1[:, slot, ts(j, 128)], ident[:, :], start=True,
                    stop=True), inc=(u == 3))
            P.wait("vector", tkT)
            tkE = P.op("vector", lambda v, b0=b0, gl=gl, bk5=bk5:
                       v.tensor_copy(YTB[:, b0:b0 + 4, ts(gl, 128)],
                                     BK(bk5, 0, 512).rearrange(
                                         "p (a b) -> p a b", a=4)), inc=True)
            bev7[bk5] = tkE
        return tkE

    prev_store7 = None
    for qg in range(n7q):
        P.wait("sync", prev_store7)
        t_xt = P.dma("sync", XT7[:], x1park_d.rearrange(
            "(c p) t -> p c t", p=128)[:, :, ds(qg * 512, 512)], sem="dx")
        prev = None
        last_yt = None
        for gl in range(n7t):
            gi = qg * 4 + gl
            slot = gi % 2
            stA = l1_frontA(gi, slot)
            if prev is not None:
                last_yt = l1_back(*prev)
            fr = l1_frontB(gi, slot, stA)
            prev = (gi, slot, fr)
        last_yt = l1_back(*prev)
        P.wait("tensor", last_yt)
        P.wait("tensor", st7["tS1"])
        P.wait("tensor", t_wo1)
        st7["tS1"] = st7["tMx"] = st7["tTr"] = st7["tY"] = None
        prev_tX2 = None
        bank_guard7 = {0: None, 1: None}
        for fc in range(EC):
            bki = fc % 2
            P.wait("tensor", bank_guard7[bki])
            tWo = None
            for j in range(EC):
                tWo = P.op("tensor", lambda t, fc=fc, j=j, bki=bki: t.matmul(
                    BK(bki), WOB[:, j, ts(fc, 128)], YTB[:, j, :],
                    start=(j == 0), stop=(j == EC - 1)), inc=(j == EC - 1))
            P.wait("scalar", tWo)
            P.wait("scalar", prev_tX2)
            sslot = fc % 2
            tS_ = P.op("scalar", lambda s, fc=fc, bki=bki, sslot=sslot:
                       s.activation(SCH7[:, sslot, :], BK(bki), AF.Identity,
                                    bias=bL1[:, 3, fc:fc + 1]), inc=True)
            bank_guard7[bki] = tS_
            P.wait("vector", tS_)
            P.wait("vector", t_xt)
            tGm7 = P.op("vector", lambda v, fc=fc, sslot=sslot, qg=qg:
                        v.tensor_mul(SCH7[:, sslot, :], SCH7[:, sslot, :],
                                     G[:, fc, ds(qg * 512, 512)]), inc=True)
            P.wait_self("vector", tGm7)
            tX2 = P.op("vector", lambda v, fc=fc, sslot=sslot: v.tensor_add(
                XT7[:, fc, :], XT7[:, fc, :], SCH7[:, sslot, :]), inc=True)
            prev_tX2 = tX2
            P.wait("sync", tX2)
            prev_store7 = P.dma(
                "sync", out_d.rearrange("(c p) t -> p c t", p=128)[
                    :, fc, ds(qg * 512, 512)], XT7[:, fc, :])
        P.wait("tensor", tS_)
        st7["tRes"] = tX2

    for dsem in Prog.DMA_SEMS:
        fin = P.sems[dsem][1]
        if fin:
            h = P.sems[dsem][0]
            P.queues["sync"].append(lambda e, h=h, fin=fin: e.wait_ge(h, fin))

    P.replay()
    return nc, sorted(tap_d.keys())


def stage_inputs(inputs):
    x = np.asarray(inputs["L0_attn_x"])
    ids = np.asarray(inputs["L1_ffn_ids"]).astype(np.int64)
    Wsel = np.ascontiguousarray(np.asarray(inputs["L1_ffn_W"])[ids])
    bsel = np.ascontiguousarray(np.asarray(inputs["L1_ffn_b"])[ids])
    w8 = np.ascontiguousarray(
        np.asarray(inputs["L1_ffn_weights"]).reshape(1, 8))

    def bias_pack(*names):
        out = np.empty((128, 4, EC), dtype=np.float32)
        for i, nm in enumerate(names):
            b = np.asarray(inputs[nm], dtype=np.float32).reshape(EC, 128)
            out[:, i, :] = b.T
        return np.ascontiguousarray(out)

    shared = {
        "Wq0": np.ascontiguousarray(inputs["L0_attn_Wq"], dtype=np.float32),
        "Wk0": np.ascontiguousarray(inputs["L0_attn_Wk"], dtype=np.float32),
        "Wv0": np.ascontiguousarray(inputs["L0_attn_Wv"], dtype=np.float32),
        "Wo0": np.ascontiguousarray(inputs["L0_attn_Wo"], dtype=np.float32),
        "bL0": bias_pack("L0_attn_bq", "L0_attn_bk", "L0_attn_bv",
                         "L0_attn_bo"),
        "wg": np.ascontiguousarray(inputs["L0_ffn_w_gate"], dtype=np.int32),
        "wu": np.ascontiguousarray(inputs["L0_ffn_w_up"], dtype=np.int32),
        "wd": np.ascontiguousarray(inputs["L0_ffn_w_down"], dtype=np.int32),
        "Wq1": np.ascontiguousarray(inputs["L1_attn_Wq"], dtype=np.float32),
        "Wk1": np.ascontiguousarray(inputs["L1_attn_Wk"], dtype=np.float32),
        "Wv1": np.ascontiguousarray(inputs["L1_attn_Wv"], dtype=np.float32),
        "Wo1": np.ascontiguousarray(inputs["L1_attn_Wo"], dtype=np.float32),
        "bL1": bias_pack("L1_attn_bq", "L1_attn_bk", "L1_attn_bv",
                         "L1_attn_bo"),
        "Wsel": Wsel.astype(np.float32),
        "bsel": bsel.astype(np.float32),
        "w8": w8.astype(np.float32),
    }
    in_maps = []
    for c in range(8):
        m = dict(shared)
        m["xT"] = np.ascontiguousarray(
            np.asarray(x[c], dtype=np.float32).T)
        in_maps.append(m)
    return in_maps


_NC_CACHE = {}


def kernel(**inputs):
    if "main" not in _NC_CACHE:
        _NC_CACHE["main"] = build()
    nc, _ = _NC_CACHE["main"]
    in_maps = stage_inputs(inputs)
    res = run_bass_kernel_spmd(nc, in_maps, core_ids=list(range(8)))
    out = np.stack([np.asarray(r["outT"]).T for r in res.results], axis=0)
    return np.ascontiguousarray(out.astype(np.float32))
